# version 19
# baseline (speedup 1.0000x reference)
import numpy as np
from contextlib import ExitStack

import concourse.bass as bass
import concourse.mybir as mybir
from concourse.bass_utils import run_bass_kernel_spmd

F32 = mybir.dt.float32
BF16 = mybir.dt.bfloat16
AF = mybir.ActivationFunctionType
ALU = mybir.AluOpType
AX = mybir.AxisListType

NCORES = 8
D = 1024
TPB = 1024
NT = 8
NP = 2048
NEXP = 8
EPS = 1e-5
INV_D = 1.0 / D
ATT_SCALE = 1.0 / 32.0

ENGINES = ("sync", "scalar", "vector", "tensor", "gpsimd")


class Arena:
    def __init__(self, cols):
        self.cols = cols
        self.free_list = [(0, cols)]
        self.allocs = {}
        self.dead = {}

    def lookup(self, name):
        return self.allocs.get(name) or self.dead[name]

    def alloc(self, name, n, align=64, side="left"):
        assert name not in self.allocs, name
        blocks = enumerate(self.free_list)
        if side == "right":
            blocks = reversed(list(blocks))
        for i, (off, sz) in blocks:
            if side == "right":
                aoff = (off + sz - n) // align * align
                if aoff < off:
                    continue
            else:
                aoff = (off + align - 1) // align * align
                if aoff + n > off + sz:
                    continue
            repl = [p for p in ((off, aoff - off),
                                (aoff + n, off + sz - (aoff + n)))
                    if p[1] > 0]
            self.free_list[i:i + 1] = repl
            self.allocs[name] = (aoff, n)
            return aoff
        raise MemoryError(f"arena alloc failed: {name} ({n} cols); "
                          f"free={self.free_list}")

    def free(self, name):
        off, n = self.allocs.pop(name)
        self.dead[name] = (off, n)
        self.free_list.append((off, n))
        self.free_list.sort()
        merged = []
        for o, s in self.free_list:
            if merged and merged[-1][0] + merged[-1][1] == o:
                merged[-1] = (merged[-1][0], merged[-1][1] + s)
            else:
                merged.append((o, s))
        self.free_list = merged


class Emitter:
    def __init__(self):
        self.ops = {e: [] for e in ENGINES}
        self.count = {}
        self.last = {}

    def val(self, sem):
        return self.count.get(sem, 0)

    def emit(self, eng, fn, waits=(), inc=None, amt=1):
        waits = tuple((s, v) for s, v in waits if v > 0)
        newval = None
        if inc is not None:
            newval = self.count.get(inc, 0) + amt
            self.count[inc] = newval

        def runner(e, sems, fn=fn, waits=waits, inc=inc, amt=amt, eng=eng):
            for s, v in waits:
                e.wait_ge(sems[s], v)
            ins = fn(e)
            self.last[eng] = ins
            if inc is not None:
                ins.then_inc(sems[inc], amt)

        self.ops[eng].append(runner)
        return newval

    def tag_last(self, eng, inc, amt=1):
        newval = self.count.get(inc, 0) + amt
        self.count[inc] = newval

        def runner(e, sems, inc=inc, amt=amt, eng=eng):
            self.last[eng].then_inc(sems[inc], amt)
        self.ops[eng].append(runner)
        return newval

    def wait(self, eng, waits):
        waits = tuple((s, v) for s, v in waits if v > 0)

        def runner(e, sems, waits=waits):
            for s, v in waits:
                e.wait_ge(sems[s], v)
        self.ops[eng].append(runner)


def build_nc():
    nc = bass.Bass("TRN2", target_bir_lowering=False, debug=False)

    x_in = nc.dram_tensor("x_in", [2 * TPB, D], F32, kind="ExternalInput").ap()
    qkv_wT = nc.dram_tensor("qkv_wT", [D, 3 * D], F32, kind="ExternalInput").ap()
    qkv_b = nc.dram_tensor("qkv_b", [1, 3 * D], F32, kind="ExternalInput").ap()
    proj_wT = nc.dram_tensor("proj_wT", [D, D], F32, kind="ExternalInput").ap()
    proj_b = nc.dram_tensor("proj_b", [1, D], F32, kind="ExternalInput").ap()
    norm_w = nc.dram_tensor("norm_w", [1, D], F32, kind="ExternalInput").ap()
    norm_b = nc.dram_tensor("norm_b", [1, D], F32, kind="ExternalInput").ap()
    phi2 = nc.dram_tensor("phi2", [D, NP], F32, kind="ExternalInput").ap()
    w_exp = nc.dram_tensor("w_exp", [NEXP, D, D], F32, kind="ExternalInput").ap()
    fb_in = nc.dram_tensor("fb_in", [NEXP, D], F32, kind="ExternalInput").ap()

    out_d = nc.dram_tensor("out", [2 * TPB, D], F32, kind="ExternalOutput").ap()

    kv_ar_in = nc.dram_tensor("kv_ar_in", [2 * D, D], BF16).ap()
    kv_ar_out = nc.dram_tensor("kv_ar_out", [2 * D, D], BF16,
                               addr_space="Shared").ap()
    sl_rs_in = [nc.dram_tensor(f"sl_rs_in{b}", [NP, D], BF16).ap()
                for b in (0, 1)]
    sl_rs_out = [nc.dram_tensor(f"sl_rs_out{b}", [NP // NCORES, D], BF16).ap()
                 for b in (0, 1)]
    y_ag_in = nc.dram_tensor("y_ag_in", [512, D], BF16).ap()
    y_ag_out = nc.dram_tensor("y_ag_out", [2 * NP, D], BF16,
                              addr_space="Shared").ap()
    comb_d = nc.dram_tensor("comb_d", [2 * TPB, NP], BF16).ap()
    x1_d = nc.dram_tensor("x1_d", [2 * TPB, D], F32).ap()

    rg = [list(range(NCORES))]

    em = Emitter()
    fa = Arena(14336)
    ba = Arena(77440)

    with ExitStack() as ctx:
        AF32 = ctx.enter_context(nc.sbuf_tensor("AF32", [128, fa.cols], F32))
        ABF = ctx.enter_context(nc.sbuf_tensor("ABF", [128, ba.cols], BF16))
        PS = [ctx.enter_context(nc.psum_tensor(f"PS{i}", [128, 1024], F32))
              for i in range(4)]
        sems = {}
        for s in ("xL0", "xL1", "xL2", "xL3", "tpH00", "tpH01", "tpH10", "tpH11", "tpX",
                  "wL", "wb0", "wb1", "kvD00", "kvD01", "kvD10", "kvD11", "spD", "cD",
                  "slD0", "slD1", "slD2", "slD3", "yD0", "yD1", "yD2", "yD3", "kvL0", "kvL1",
                  "x1R0", "x1R1", "yB", "slL", "tpC", "O0", "O1",
                  "T", "V", "S", "G", "C"):
            sems[s] = ctx.enter_context(nc.semaphore(s))
        block = ctx.enter_context(nc.Block())

        def f32(name):
            off, n = fa.lookup(name)
            return AF32[:, off:off + n]

        def bf(name):
            off, n = ba.lookup(name)
            return ABF[:, off:off + n]

        E = em.emit

        fa.alloc("COLS", 256)
        fa.alloc("ROWSF", 2048)
        fa.alloc("ONESF", 128)
        _colreg = {"next": 0, "map": {}}

        def cols(name, n):
            if name not in _colreg["map"]:
                _colreg["map"][name] = (_colreg["next"], n)
                _colreg["next"] += n
                assert _colreg["next"] <= 256
            off, nn = _colreg["map"][name]
            assert nn == n
            c0, _ = fa.lookup("COLS")
            return AF32[:, c0 + off:c0 + off + n]

        def rowsf(a, b_):
            off, _ = fa.lookup("ROWSF")
            return AF32[0:1, off + a:off + b_]

        def onesf():
            off, _ = fa.lookup("ONESF")
            return AF32[0:1, off:off + 128]

        ba.alloc("ONES", 128)
        ba.alloc("SINK", D)
        ba.alloc("ROWS", 4096)

        def ones():
            off, _ = ba.lookup("ONES")
            return ABF[0:1, off:off + 128]

        def rows(a, b_):
            off, _ = ba.lookup("ROWS")
            return ABF[0:1, off + a:off + b_]

        ba.alloc("X1LN", 16 * D, side="right")
        o_T = {}
        ba.alloc("QW_KV", 8 * 2 * D)

        E("gpsimd", lambda e: e.memset(ones(), 1.0), inc="G")
        E("gpsimd", lambda e: e.memset(onesf(), 1.0), inc="G")
        eps_col = cols("eps", 1)
        E("gpsimd", lambda e: e.memset(eps_col, EPS), inc="G")
        eps2_col = cols("eps2", 1)
        E("gpsimd", lambda e: e.memset(eps2_col, EPS * EPS), inc="G")
        ones_done = em.val("G")

        E("gpsimd", lambda e: e.dma_start(out=rows(0, 3 * D), in_=qkv_b),
          inc="wL", amt=16)
        E("gpsimd", lambda e: e.dma_start(out=rows(3 * D, 4 * D), in_=proj_b),
          inc="wL", amt=16)
        E("gpsimd", lambda e: e.dma_start(out=rowsf(0, D), in_=norm_w),
          inc="wL", amt=16)
        E("gpsimd", lambda e: e.dma_start(out=rowsf(D, 2 * D), in_=norm_b),
          inc="wL", amt=16)
        qb_col = cols("qb", 8)
        E("gpsimd", lambda e: e.dma_start(
            out=qb_col, in_=qkv_b.rearrange("1 (j p) -> p j", p=128)[:, 0:8],
            allow_slow_non_contiguous=True), inc="wL", amt=16)
        rows_done = em.val("wL")

        qw_chunk = {}
        for c in range(8):
            qw_chunk[c] = E("gpsimd", lambda e, c=c: e.dma_start(
                out=bf("QW_KV")[:, c * 2 * D:(c + 1) * 2 * D],
                in_=qkv_wT[c * 128:(c + 1) * 128, D:3 * D]),
                waits=[("wL", qw_chunk[c - 1])] if c > 0 else
                [("wL", rows_done)],
                inc="wL", amt=16)
        qw_done = em.val("wL")

        fa.alloc("SCR", 4 * D)
        fa.alloc("WB", D)
        fa.alloc("BB", D)
        E("tensor", lambda e: e.matmul(
            PS[0][:, 0:512], onesf(), rowsf(0, 512), start=True, stop=True),
            waits=[("wL", rows_done), ("G", ones_done)])
        E("tensor", lambda e: e.matmul(
            PS[0][:, 512:1024], onesf(), rowsf(512, D), start=True, stop=True))
        E("tensor", lambda e: e.matmul(
            PS[1][:, 0:512], onesf(), rowsf(D, D + 512), start=True,
            stop=True))
        tv = E("tensor", lambda e: e.matmul(
            PS[1][:, 512:1024], onesf(), rowsf(D + 512, 2 * D), start=True,
            stop=True), inc="T")
        E("vector", lambda e: e.tensor_copy(f32("WB"), PS[0][:, :]),
          waits=[("T", tv)], inc="V")
        bcast_done = E("vector", lambda e: e.tensor_copy(
            f32("BB"), PS[1][:, :]), inc="V")
        fa.alloc("KBB", D)
        fa.alloc("VBB", D)
        E("tensor", lambda e: e.matmul(
            PS[2][:, 0:512], ones(), rows(D, D + 512), start=True, stop=True))
        E("tensor", lambda e: e.matmul(
            PS[2][:, 512:1024], ones(), rows(D + 512, 2 * D), start=True,
            stop=True))
        E("tensor", lambda e: e.matmul(
            PS[3][:, 0:512], ones(), rows(2 * D, 2 * D + 512), start=True,
            stop=True))
        tv2 = E("tensor", lambda e: e.matmul(
            PS[3][:, 512:1024], ones(), rows(2 * D + 512, 3 * D), start=True,
            stop=True), inc="T")
        E("vector", lambda e: e.tensor_copy(f32("KBB"), PS[2][:, :]),
          waits=[("T", tv2)], inc="V")
        kvb_done = E("vector", lambda e: e.tensor_copy(
            f32("VBB"), PS[3][:, :]), inc="V")

        ps_guard = [("V", bcast_done), ("V", bcast_done),
                    ("V", kvb_done), ("V", kvb_done)]

        fa.alloc("XSTG", 3 * D)
        ba.alloc("KVST", 2 * D)

        xstg_guard = [("V", 0)] * 3

        def xstg(i):
            off, _ = fa.lookup("XSTG")
            i = i % 3
            return AF32[:, off + i * D: off + (i + 1) * D]

        def scr(i):
            off, _ = fa.lookup("SCR")
            i = i % 4
            return AF32[:, off + i * D: off + (i + 1) * D]

        def kvst(i):
            off, _ = ba.lookup("KVST")
            i = i % 2
            return ABF[:, off + i * D: off + (i + 1) * D]

        def load_x(slot, gt, extra=()):
            w = [xstg_guard[slot % 3]] + list(extra)
            sem = f"xL{slot % 4}"
            v = E("scalar", lambda e, slot=slot, gt=gt: e.dma_start(
                out=xstg(slot), in_=x_in[gt * 128:(gt + 1) * 128, :]),
                waits=w, inc=sem, amt=16)
            return (sem, v)

        rstd0 = cols("rstd0", 16)
        nmr0 = cols("nmr0", 16)
        nm0 = cols("nm0", 16)
        ssq0 = cols("ssq0", 16)
        var0 = cols("var0", 16)
        lncol = cols("lncol", 32)
        tmp8 = cols("tmp8", 16)
        smax = cols("smax", 8)

        qkv_ready = {}
        kvst_guard = [("kvD00", 0), ("kvD01", 0)]
        ar_done = {}
        kv_loaded = {}
        x1_spill = {}
        x1ln_done_tile = {}
        h_tp = {}

        s_sq = 0
        g_par = {0: 0, 1: 0}
        v_mul = {}
        for b in (0, 1):
            tb = b * NT
            a_off = 0 + b * 8
            c_off = 16 + b * 8
            ba.alloc(f"HT{b}", NT * D)
            ba.alloc(f"HST{b}", 2 * D)

            def hstage(i, b=b):
                off, _ = ba.lookup(f"HST{b}")
                i = i % 2
                return ABF[:, off + i * D: off + (i + 1) * D]

            for t in range(NT):
                gt = tb + t
                xl = load_x(gt, gt)
                s_sq = E("scalar", lambda e, gt=gt: e.activation(
                    bf("SINK"), xstg(gt), AF.Square, bias=0.0,
                    scale=1.0, accum_out=ssq0[:, gt:gt + 1]),
                    waits=[xl] + ([("S", s_sq)] if s_sq else []),
                    inc="S")
                v1 = E("vector", lambda e, gt=gt: e.reduce_sum(
                    nm0[:, gt:gt + 1], xstg(gt), axis=AX.X, negate=True),
                    waits=[xl], inc="V")
                v2 = E("vector", lambda e, gt=gt: e.tensor_scalar_mul(
                    nm0[:, gt:gt + 1], nm0[:, gt:gt + 1], INV_D),
                    waits=[("V", v1)], inc="V")
                v2b = E("vector", lambda e, gt=gt: e.tensor_mul(
                    tmp8[:, 8:9], nm0[:, gt:gt + 1], nm0[:, gt:gt + 1]),
                    waits=[("V", v2)], inc="V")
                v3 = E("vector", lambda e, gt=gt: e.tensor_scalar_mul(
                    var0[:, gt:gt + 1], ssq0[:, gt:gt + 1], INV_D),
                    waits=[("S", s_sq), ("V", v2b)], inc="V")
                v3 = E("vector", lambda e, gt=gt: e.tensor_sub(
                    var0[:, gt:gt + 1], var0[:, gt:gt + 1], tmp8[:, 8:9]),
                    waits=[("V", v3)], inc="V")
                s2 = E("scalar", lambda e, gt=gt: e.activation(
                    tmp8[:, 6:7], var0[:, gt:gt + 1], AF.Sqrt,
                    bias=eps2_col, scale=1.0 + EPS),
                    waits=[("V", v3)], inc="S")
                v6 = E("vector", lambda e, gt=gt, t=t, a_off=a_off:
                       e.reciprocal(
                        lncol[:, a_off + t:a_off + t + 1], tmp8[:, 6:7]),
                    waits=[("S", s2)], inc="V")
                v7 = E("vector", lambda e, gt=gt, t=t, a_off=a_off,
                       c_off=c_off: e.tensor_mul(
                        lncol[:, c_off + t:c_off + t + 1],
                        nm0[:, gt:gt + 1],
                        lncol[:, a_off + t:a_off + t + 1]),
                    waits=[("V", v6)], inc="V")
                s2b = E("scalar", lambda e, gt=gt: e.activation(
                    tmp8[:, 7:8], var0[:, gt:gt + 1], AF.Sqrt,
                    bias=eps_col, scale=1.0), waits=[("V", v3)], inc="S")
                v4 = E("vector", lambda e, gt=gt: e.reciprocal(
                    rstd0[:, gt:gt + 1], tmp8[:, 7:8]),
                    waits=[("S", s2b)], inc="V")
                v5 = E("vector", lambda e, gt=gt: e.tensor_mul(
                    nmr0[:, gt:gt + 1], nm0[:, gt:gt + 1],
                    rstd0[:, gt:gt + 1]), waits=[("V", v4)], inc="V")
                iA = gt % 2
                iB = 2 + gt % 2
                s4 = E("scalar", lambda e, gt=gt, t=t, iA=iA, a_off=a_off,
                       c_off=c_off: e.activation(
                           scr(iA), xstg(gt), AF.Identity,
                           bias=lncol[:, c_off + t:c_off + t + 1],
                           scale=lncol[:, a_off + t:a_off + t + 1]),
                       waits=[("V", v7)] +
                       ([("V", v_mul[gt - 2])] if gt >= 2 else []), inc="S")
                xstg_guard[gt % 3] = ("S", s4)
                v11 = E("vector", lambda e, iA=iA, iB=iB: e.tensor_mul(
                    scr(iB), scr(iA), f32("WB")),
                    waits=[("S", s4), ("V", bcast_done)] +
                    ([("G", g_par[gt % 2])] if g_par[gt % 2] else []),
                    inc="V")
                v_mul[gt] = v11
                g1 = E("gpsimd", lambda e, t=t, iB=iB, hstage=hstage:
                       e.tensor_add(hstage(t), scr(iB), f32("BB")),
                    waits=[("V", v11), ("G", ones_done)] +
                    ([h_tp[(b, t - 2)]] if t >= 2 else []), inc="G")
                g_par[gt % 2] = g1
                tsem = f"tpH{b}{t % 2}"
                E("sync", lambda e, t=t, b=b, hstage=hstage: e.dma_start(
                    out=bf(f"HT{b}").rearrange("p (c q) -> p c q", c=8)
                    [:, :, t * 128:(t + 1) * 128],
                    in_=hstage(t).rearrange("p (c q) -> p c q", c=8),
                    transpose=True),
                    waits=[("G", g1)], inc=tsem, amt=16)
                h_tp[(b, t)] = (tsem, em.val(tsem))

            tb = b * NT
            ba.alloc(f"K{b}", NT * D)
            ba.alloc(f"V{b}", NT * D)
            kv_evac_v = 0
            kv_evac_s = 0
            for t in range(NT):
                g0, g1_ = ps_guard[0], ps_guard[1]
                first = True
                for c in range(8):
                    lhs = bf(f"HT{b}")[:, c * D + t * 128: c * D + t * 128 + 128]
                    for psn, fb0 in ((0, 0), (1, D)):
                        for n in range(2):
                            if first:
                                w = [h_tp[(b, t)], ("wL", rows_done),
                                     ("V", kvb_done), g0, g1_]
                                if b == 0 and t == 0:
                                    w.append(("wL", qw_chunk[c]))
                                first = False
                            elif (b == 0 and t == 0 and psn == 0
                                  and n == 0):
                                w = [("wL", qw_chunk[c])]
                            else:
                                w = ()
                            E("tensor", lambda e, lhs=lhs, psn=psn, c=c,
                              fb0=fb0, n=n: e.matmul(
                                PS[psn][:, n * 512:(n + 1) * 512], lhs,
                                bf("QW_KV")[:, c * 2 * D + fb0 + n * 512:
                                            c * 2 * D + fb0 + (n + 1) * 512],
                                start=(c == 0), stop=(c == 7)),
                                waits=w)
                tK = em.tag_last("tensor", "T")
                kv_evac_v = E("vector", lambda e, t=t, b=b: e.tensor_add(
                    bf(f"K{b}")[:, t * D:(t + 1) * D], PS[0][:, :],
                    f32("KBB")),
                    waits=[("T", tK)], inc="V")
                kv_evac_s = E("vector", lambda e, t=t, b=b: e.tensor_add(
                    bf(f"V{b}")[:, t * D:(t + 1) * D], PS[1][:, :],
                    f32("VBB")),
                    waits=[("V", kv_evac_v)], inc="V")
                ps_guard[0] = ("V", kv_evac_v)
                ps_guard[1] = ("V", kv_evac_s)

            kv_dma = {}
            for c1 in range(8):
                psn = 2 + (c1 % 2)
                g = ps_guard[psn]
                for n in range(2):
                    for t in range(NT):
                        E("tensor", lambda e, c1=c1, n=n, t=t, psn=psn, b=b:
                          e.matmul(
                            PS[psn][:, n * 512:(n + 1) * 512],
                            bf(f"K{b}")[:, t * D + c1 * 128:
                                        t * D + c1 * 128 + 128],
                            bf(f"V{b}")[:, t * D + n * 512:
                                        t * D + (n + 1) * 512],
                            start=(t == 0), stop=(t == NT - 1)),
                            waits=[("V", kv_evac_s), g]
                            if (n == 0 and t == 0) else ())
                tK = em.tag_last("tensor", "T")
                vv = E("vector", lambda e, c1=c1, psn=psn: e.tensor_copy(
                    kvst(c1), PS[psn][:, :]),
                    waits=[("T", tK), kvst_guard[c1 % 2]], inc="V")
                ps_guard[psn] = ("V", vv)
                ksem = f"kvD{b}{c1 % 2}"
                kv_dma[c1] = E("scalar", lambda e, c1=c1, b=b: e.dma_start(
                    out=kv_ar_in[b * D + c1 * 128: b * D + (c1 + 1) * 128, :],
                    in_=kvst(c1)), waits=[("V", vv)], inc=ksem, amt=16)
                kvst_guard[c1 % 2] = (ksem, kv_dma[c1])

            ar_done[b] = E("gpsimd", lambda e, b=b: e.collective_compute(
                "AllReduce", ALU.add, replica_groups=rg,
                ins=[kv_ar_in[b * D:(b + 1) * D, :]],
                outs=[kv_ar_out[b * D:(b + 1) * D, :]]),
                waits=[(f"kvD{b}0", em.val(f"kvD{b}0")),
                       (f"kvD{b}1", em.val(f"kvD{b}1"))],
                inc="C")

            ba.free(f"HST{b}")
            ba.free(f"K{b}")
            ba.free(f"V{b}")

        ba.free("QW_KV")
        kvpart_T = em.val("T")
        ba.alloc("QW_Q", 8 * D)
        for c in range(8):
            E("gpsimd", lambda e, c=c: e.dma_start(
                out=bf("QW_Q")[:, c * D:(c + 1) * D],
                in_=qkv_wT[c * 128:(c + 1) * 128, 0:D]),
                waits=[("T", kvpart_T)] if c == 0 else (), inc="wL", amt=16)
        qwq_done = em.val("wL")
        for b in (0, 1):
            ba.alloc(f"QT{b}", NT * D)
            for f in range(8):
                psn = 2 + (f % 2)
                g = ps_guard[psn]
                for c in range(8):
                    lhs = bf("QW_Q")[:, c * D + f * 128: c * D + f * 128 + 128]
                    for n in range(2):
                        E("tensor", lambda e, lhs=lhs, psn=psn, c=c, n=n, b=b:
                          e.matmul(
                            PS[psn][:, n * 512:(n + 1) * 512], lhs,
                            bf(f"HT{b}")[:, c * D + n * 512:
                                         c * D + (n + 1) * 512],
                            start=(c == 0), stop=(c == 7)),
                            waits=[g, ("wL", qwq_done)]
                            if (c == 0 and n == 0) else ())
                tK = em.tag_last("tensor", "T")
                sv = E("scalar", lambda e, f=f, psn=psn, b=b: e.activation(
                    bf(f"QT{b}")[:, f * D:(f + 1) * D], PS[psn][:, :],
                    AF.Identity, bias=qb_col[:, f:f + 1], scale=1.0),
                    waits=[("T", tK)], inc="S")
                ps_guard[psn] = ("S", sv)
            qkv_ready[b] = em.val("S")
        ba.free("HT0")
        ba.free("HT1")
        ba.free("QW_Q")
        q_T = em.val("T")

        for b in (0, 1):
            ba.alloc(f"KV{b}", 8 * D)
            kv_loaded[b] = (f"kvL{b}", E(
                "gpsimd", lambda e, b=b: e.dma_start(
                    out=bf(f"KV{b}").rearrange("p (c d) -> p c d", d=D),
                    in_=kv_ar_out.rearrange(
                        "(b c p) d -> b p c d", b=2, p=128)[b]),
                waits=[("C", ar_done[b]), ("S", qkv_ready[b]), ("T", q_T)],
                inc=f"kvL{b}", amt=16))

        ba.alloc("PW", 8 * D)
        for c in range(8):
            E("gpsimd", lambda e, c=c: e.dma_start(
                out=bf("PW")[:, c * D:(c + 1) * D],
                in_=proj_wT[c * 128:(c + 1) * 128, :]),
                waits=[("T", q_T)] if c == 0 else (),
                inc="wL", amt=16)
        pw_done = em.val("wL")

        for b in (0, 1):
            tb = b * NT
            ba.alloc(f"OT{b}", NT * D)
            for c2 in range(8):
                psn = c2 % 2
                g = ps_guard[psn]
                for c1 in range(8):
                    lhs = bf(f"KV{b}")[:, c1 * D + c2 * 128:
                                       c1 * D + c2 * 128 + 128]
                    for n in range(2):
                        E("tensor", lambda e, lhs=lhs, psn=psn, c1=c1, n=n,
                          b=b: e.matmul(
                            PS[psn][:, n * 512:(n + 1) * 512], lhs,
                            bf(f"QT{b}")[:, c1 * D + n * 512:
                                         c1 * D + (n + 1) * 512],
                            start=(c1 == 0), stop=(c1 == 7)),
                            waits=[kv_loaded[b], g]
                            if (c1 == 0 and n == 0) else ())
                tK = em.tag_last("tensor", "T")
                sv = E("scalar", lambda e, c2=c2, psn=psn, b=b: e.activation(
                    bf(f"OT{b}")[:, c2 * D:(c2 + 1) * D], PS[psn][:, :],
                    AF.Identity, bias=0.0, scale=ATT_SCALE),
                    waits=[("T", tK)], inc="S")
                ps_guard[psn] = ("S", sv)
            ot_ready = em.val("S")
            o_T[b] = em.val("T")

            for t in range(NT):
                gt = tb + t
                psn = 2 + (t % 2)
                g = ps_guard[psn]
                xl = load_x(gt, gt)
                E("tensor", lambda e, psn=psn: e.matmul(
                    PS[psn][:, 0:512], ones(), rows(3 * D, 3 * D + 512),
                    start=True, stop=False),
                    waits=[g, ("S", ot_ready), ("wL", pw_done)])
                E("tensor", lambda e, psn=psn: e.matmul(
                    PS[psn][:, 512:1024], ones(), rows(3 * D + 512, 4 * D),
                    start=True, stop=False))
                for c2 in range(8):
                    lhs = bf(f"OT{b}")[:, c2 * D + t * 128:
                                       c2 * D + t * 128 + 128]
                    for n in range(2):
                        E("tensor", lambda e, lhs=lhs, psn=psn, c2=c2, n=n:
                          e.matmul(
                            PS[psn][:, n * 512:(n + 1) * 512], lhs,
                            bf("PW")[:, c2 * D + n * 512:
                                     c2 * D + (n + 1) * 512],
                            start=False, stop=(c2 == 7)))
                tK = em.tag_last("tensor", "T")
                s1 = E("scalar", lambda e, gt=gt: e.activation(
                    scr(0), xstg(gt), AF.Identity, bias=nmr0[:, gt:gt + 1],
                    scale=rstd0[:, gt:gt + 1]),
                    waits=[("T", tK), xl], inc="S")
                v1 = E("vector", lambda e, psn=psn: e.tensor_add(
                    scr(1), PS[psn][:, :], scr(0)),
                    waits=[("S", s1), ("T", tK)], inc="V")
                ps_guard[psn] = ("V", v1)
                v2 = E("vector", lambda e: e.reduce_sum(
                    tmp8[:, 0:1], scr(1), axis=AX.X, negate=True),
                    waits=[("V", v1)], inc="V")
                v3 = E("vector", lambda e: e.tensor_scalar_mul(
                    tmp8[:, 0:1], tmp8[:, 0:1], INV_D),
                    waits=[("V", v2)], inc="V")
                s2 = E("scalar", lambda e: e.activation(
                    scr(0), scr(1), AF.Square, bias=tmp8[:, 0:1], scale=1.0,
                    accum_out=tmp8[:, 1:2]), waits=[("V", v3)], inc="S")
                v4 = E("vector", lambda e: e.tensor_scalar_mul(
                    tmp8[:, 1:2], tmp8[:, 1:2], INV_D),
                    waits=[("S", s2)], inc="V")
                s3 = E("scalar", lambda e: e.activation(
                    tmp8[:, 1:2], tmp8[:, 1:2], AF.Sqrt, bias=eps_col, scale=1.0),
                    waits=[("V", v4)], inc="S")
                v5 = E("vector", lambda e: e.reciprocal(
                    tmp8[:, 1:2], tmp8[:, 1:2]), waits=[("S", s3)], inc="V")
                v6 = E("vector", lambda e: e.tensor_mul(
                    tmp8[:, 0:1], tmp8[:, 0:1], tmp8[:, 1:2]),
                    waits=[("V", v5)], inc="V")
                s4 = E("scalar", lambda e: e.activation(
                    scr(2), scr(1), AF.Identity, bias=tmp8[:, 0:1],
                    scale=tmp8[:, 1:2]), waits=[("V", v6)], inc="S")
                v7 = E("gpsimd", lambda e, gt=gt: e.tensor_add(
                    scr(3), xstg(gt), scr(2)),
                    waits=[("S", s4)] +
                    ([("spD", x1_spill[gt - 1])] if gt >= 1 else []), inc="G")
                xstg_guard[gt % 3] = ("G", v7)
                x1_spill[gt] = E("scalar", lambda e, gt=gt: e.dma_start(
                    out=x1_d[gt * 128:(gt + 1) * 128, :], in_=scr(3)),
                    waits=[("G", v7)], inc="spD", amt=16)
                v8 = E("vector", lambda e: e.reduce_sum(
                    tmp8[:, 2:3], scr(3), axis=AX.X, negate=True),
                    waits=[("G", v7)], inc="V")
                v9 = E("vector", lambda e: e.tensor_scalar_mul(
                    tmp8[:, 2:3], tmp8[:, 2:3], INV_D),
                    waits=[("V", v8)], inc="V")
                s5 = E("scalar", lambda e: e.activation(
                    scr(0), scr(3), AF.Square, bias=tmp8[:, 2:3], scale=1.0,
                    accum_out=tmp8[:, 3:4]), waits=[("V", v9)], inc="S")
                v10 = E("vector", lambda e: e.tensor_scalar_mul(
                    tmp8[:, 3:4], tmp8[:, 3:4], INV_D),
                    waits=[("S", s5)], inc="V")
                s6 = E("scalar", lambda e: e.activation(
                    tmp8[:, 3:4], tmp8[:, 3:4], AF.Sqrt, bias=eps_col, scale=1.0),
                    waits=[("V", v10)], inc="S")
                v11 = E("vector", lambda e: e.reciprocal(
                    tmp8[:, 3:4], tmp8[:, 3:4]), waits=[("S", s6)], inc="V")
                v12 = E("vector", lambda e: e.tensor_mul(
                    tmp8[:, 2:3], tmp8[:, 2:3], tmp8[:, 3:4]),
                    waits=[("V", v11)], inc="V")
                x1ln_done_tile[gt] = E("scalar", lambda e, gt=gt: e.activation(
                    bf("X1LN")[:, gt * D:(gt + 1) * D], scr(3), AF.Identity,
                    bias=tmp8[:, 2:3], scale=tmp8[:, 3:4]),
                    waits=[("V", v12)], inc="S")

            ba.free(f"OT{b}")
            ba.free(f"QT{b}")
            ba.free(f"KV{b}")

        x1ln_all = em.val("S")
        attn_T = em.val("T")
        ba.free("PW")
        ba.free("ROWS")
        ba.free("KVST")
        fa.free("XSTG")
        fa.free("WB")
        fa.free("BB")
        fa.free("KBB")
        fa.free("VBB")

        ba.alloc("PHI", 8 * NP, side="right")
        for c in range(8):
            E("gpsimd", lambda e, c=c: e.dma_start(
                out=bf("PHI")[:, c * NP:(c + 1) * NP],
                in_=phi2[c * 128:(c + 1) * 128, :]),
                waits=[("T", o_T[1])] if c == 0 else (),
                inc="wL", amt=16)
        phi_done = em.val("wL")

        ba.alloc("XTS", 2 * D)
        fa.alloc("ESCR", 3 * NP)
        ba.alloc("L1", NT * NP)
        ba.alloc("DIFF", NT * NP)
        ba.alloc("CSTG", NP)

        def xts(i):
            off, _ = ba.lookup("XTS")
            i = i % 2
            return ABF[:, off + i * D: off + (i + 1) * D]

        def escr(i):
            off, _ = fa.lookup("ESCR")
            i = i % 3
            return AF32[:, off + i * NP: off + (i + 1) * NP]

        def cstg():
            off, _ = ba.lookup("CSTG")
            return ABF[:, off:off + NP]

        xt_tp = {}
        logits_tK = {}
        comb_dma = {}

        def emit_xT(seq, gt):
            waits = [("S", x1ln_done_tile[gt]), ("T", attn_T)]
            if seq >= 2:
                waits.append(("T", logits_tK[seq - 2]))
            tsem = f"tpH0{seq % 2}"
            E("sync", lambda e, seq=seq, gt=gt: e.dma_start(
                out=xts(seq).rearrange("p (c q) -> p c q", c=8),
                in_=bf("X1LN").rearrange("p (g c q) -> p g c q", g=16, c=8)
                [:, gt, :, :],
                transpose=True), waits=waits, inc=tsem, amt=16)
            xt_tp[seq] = (tsem, em.val(tsem))

        def logits_mm(seq, pa, pb, extra_waits):
            for c in range(8):
                lhs = xts(seq)[:, c * 128:(c + 1) * 128]
                for n in range(4):
                    psn = pa if n < 2 else pb
                    E("tensor", lambda e, lhs=lhs, psn=psn, c=c, n=n: e.matmul(
                        PS[psn][:, (n % 2) * 512:((n % 2) + 1) * 512], lhs,
                        bf("PHI")[:, c * NP + n * 512: c * NP + (n + 1) * 512],
                        start=(c == 0), stop=(c == 7)),
                        waits=(list(extra_waits) + [xt_tp[seq]])
                        if (c == 0 and n == 0) else ())
            logits_tK[seq] = em.tag_last("tensor", "T")
            return logits_tK[seq]

        sC = 0
        sB = 0
        ebg = {0: 0, 1: 0}
        for t in range(NT):
            seq = t
            emit_xT(seq, NT + t)
            pa, pb = (0, 1) if t % 2 == 0 else (2, 3)
            ga, gb = ps_guard[pa], ps_guard[pb]
            tK = logits_mm(seq, pa, pb, [ga, gb, ("wL", phi_done)])
            eb = escr(t % 2)
            ex = escr(2)
            vA = E("vector", lambda e, eb=eb, pa=pa: e.tensor_copy(
                eb[:, 0:1024], PS[pa][:, :]),
                waits=[("T", tK)] + ([("S", sB)] if sB else []) +
                ([("G", ebg[t % 2])] if ebg[t % 2] else []), inc="V")
            vB = E("vector", lambda e, eb=eb, pb=pb: e.tensor_copy(
                eb[:, 1024:2048], PS[pb][:, :]), waits=[("V", vA)], inc="V")
            ps_guard[pa] = ("V", vB)
            ps_guard[pb] = ("V", vB)
            vL = E("gpsimd", lambda e, t=t, eb=eb: e.tensor_copy(
                bf("L1")[:, t * NP:(t + 1) * NP], eb),
                waits=[("V", vB)], inc="G")
            ebg[t % 2] = vL
            vC = E("vector", lambda e, eb=eb: e.reduce_max(
                smax[:, 0:1], eb[:, 0:1024], axis=AX.X),
                waits=[("V", vB)], inc="V")
            vD = E("vector", lambda e, eb=eb: e.reduce_max(
                smax[:, 1:2], eb[:, 1024:2048], axis=AX.X),
                waits=[("V", vC)], inc="V")
            vE = E("vector", lambda e: e.tensor_max(
                smax[:, 0:1], smax[:, 0:1], smax[:, 1:2]),
                waits=[("V", vD)], inc="V")
            vF = E("vector", lambda e: e.tensor_scalar_mul(
                smax[:, 1:2], smax[:, 0:1], -1.0), waits=[("V", vE)], inc="V")
            sA = E("scalar", lambda e, eb=eb, ex=ex: e.activation(
                ex[:, 0:1024], eb[:, 0:1024], AF.Exp, bias=smax[:, 1:2],
                scale=1.0, accum_out=smax[:, 2:3]),
                waits=[("V", vF)] + ([("S", sC)] if sC else []), inc="S")
            sB = E("scalar", lambda e, eb=eb, ex=ex: e.activation(
                ex[:, 1024:2048], eb[:, 1024:2048], AF.Exp, bias=smax[:, 1:2],
                scale=1.0, accum_out=smax[:, 3:4]), inc="S")
            vG = E("vector", lambda e: e.tensor_add(
                smax[:, 2:3], smax[:, 2:3], smax[:, 3:4]),
                waits=[("S", sB)], inc="V")
            vH = E("vector", lambda e: e.reciprocal(
                smax[:, 3:4], smax[:, 2:3]), waits=[("V", vG)], inc="V")
            sC = E("scalar", lambda e, ex=ex: e.activation(
                cstg(), ex, AF.Identity, bias=0.0, scale=smax[:, 3:4]),
                waits=[("V", vH)] +
                ([("cD", comb_dma[t - 1])] if t >= 1 else []), inc="S")
            comb_dma[t] = E("gpsimd", lambda e, t=t: e.dma_start(
                out=comb_d[TPB + t * 128: TPB + (t + 1) * 128, :],
                in_=cstg()), waits=[("S", sC)], inc="cD", amt=16)

        for t in range(NT):
            seq = NT + t
            emit_xT(seq, t)
            pa, pb = (0, 1) if t % 2 == 0 else (2, 3)
            ga, gb = ps_guard[pa], ps_guard[pb]
            tK = logits_mm(seq, pa, pb, [ga, gb])
            eb = escr(t % 2)
            ex = escr(2)
            vA = E("vector", lambda e, eb=eb, pa=pa: e.tensor_copy(
                eb[:, 0:1024], PS[pa][:, :]),
                waits=[("T", tK), ("S", sB), ("G", ebg[t % 2])], inc="V")
            vB = E("vector", lambda e, eb=eb, pb=pb: e.tensor_copy(
                eb[:, 1024:2048], PS[pb][:, :]), waits=[("V", vA)], inc="V")
            ps_guard[pa] = ("V", vB)
            ps_guard[pb] = ("V", vB)
            vDf = E("gpsimd", lambda e, t=t, eb=eb: e.tensor_sub(
                bf("DIFF")[:, t * NP:(t + 1) * NP], eb,
                bf("L1")[:, t * NP:(t + 1) * NP]),
                waits=[("V", vB)], inc="G")
            ebg[t % 2] = vDf
            vC = E("vector", lambda e, eb=eb: e.reduce_max(
                smax[:, 4:5], eb[:, 0:1024], axis=AX.X),
                waits=[("V", vB)], inc="V")
            vD = E("vector", lambda e, eb=eb: e.reduce_max(
                smax[:, 5:6], eb[:, 1024:2048], axis=AX.X),
                waits=[("V", vC)], inc="V")
            vE = E("vector", lambda e: e.tensor_max(
                smax[:, 4:5], smax[:, 4:5], smax[:, 5:6]),
                waits=[("V", vD)], inc="V")
            vF = E("vector", lambda e: e.tensor_scalar_mul(
                smax[:, 5:6], smax[:, 4:5], -1.0), waits=[("V", vE)], inc="V")
            sA = E("scalar", lambda e, eb=eb, ex=ex: e.activation(
                ex[:, 0:1024], eb[:, 0:1024], AF.Exp, bias=smax[:, 5:6],
                scale=1.0, accum_out=smax[:, 6:7]),
                waits=[("V", vF), ("S", sC)], inc="S")
            sB = E("scalar", lambda e, eb=eb, ex=ex: e.activation(
                ex[:, 1024:2048], eb[:, 1024:2048], AF.Exp, bias=smax[:, 5:6],
                scale=1.0, accum_out=smax[:, 7:8]), inc="S")
            vG = E("vector", lambda e: e.tensor_add(
                smax[:, 6:7], smax[:, 6:7], smax[:, 7:8]),
                waits=[("S", sB)], inc="V")
            vH = E("vector", lambda e: e.reciprocal(
                smax[:, 7:8], smax[:, 6:7]), waits=[("V", vG)], inc="V")
            sC = E("scalar", lambda e, ex=ex: e.activation(
                cstg(), ex, AF.Identity, bias=0.0, scale=smax[:, 7:8]),
                waits=[("V", vH), ("cD", comb_dma[NT + t - 1])], inc="S")
            comb_dma[NT + t] = E("gpsimd", lambda e, t=t: e.dma_start(
                out=comb_d[t * 128:(t + 1) * 128, :], in_=cstg()),
                waits=[("S", sC)], inc="cD", amt=16)
        comb_dma_done = em.val("cD")
        l0_done = em.val("S")
        diff_last_v = em.val("G")

        ba.free("PHI")
        ba.free("XTS")
        fa.free("ESCR")
        ba.free("CSTG")

        d0 = bf("L1")
        d1 = bf("DIFF")
        disp_done = 0
        for t in range(NT):
            s = E("scalar", lambda e, t=t: e.activation(
                d0[:, t * NP:(t + 1) * NP],
                bf("DIFF")[:, t * NP:(t + 1) * NP], AF.Sigmoid,
                bias=0.0, scale=1.0),
                waits=[("S", l0_done), ("G", diff_last_v)] if t == 0 else (),
                inc="S")
            disp_done = E("vector", lambda e, t=t, s=s: e.tensor_scalar(
                d1[:, t * NP:(t + 1) * NP], d0[:, t * NP:(t + 1) * NP],
                -1.0, 1.0, ALU.mult, ALU.add), waits=[("S", s)], inc="V")

        ba.alloc("SLST", 2 * D)

        def slst(i):
            off, _ = ba.lookup("SLST")
            i = i % 2
            return ABF[:, off + i * D: off + (i + 1) * D]

        sl_dma = {}
        slots_tK = {}
        idx = 0
        for b in (0, 1):
            dt_ = d0 if b == 0 else d1
            for j in range(16):
                psn = j % 2
                g = ps_guard[psn]
                for t in range(NT):
                    lhs = dt_[:, t * NP + j * 128: t * NP + j * 128 + 128]
                    for n in range(2):
                        E("tensor", lambda e, lhs=lhs, psn=psn, t=t, n=n, b=b:
                          e.matmul(
                            PS[psn][:, n * 512:(n + 1) * 512], lhs,
                            bf("X1LN")[:, (b * NT + t) * D + n * 512:
                                       (b * NT + t) * D + (n + 1) * 512],
                            start=(t == 0), stop=(t == NT - 1)),
                            waits=[g, ("V", disp_done)]
                            if (t == 0 and n == 0) else ())
                tK = em.tag_last("tensor", "T")
                ssem = f"slD{2 * b + idx % 2}"
                vv = E("vector", lambda e, idx=idx, psn=psn: e.tensor_copy(
                    slst(idx), PS[psn][:, :]),
                    waits=[("T", tK)] +
                    ([sl_dma[idx - 2]] if idx >= 2 else []), inc="V")
                ps_guard[psn] = ("V", vv)
                row = (j // 2) * 256 + (j % 2) * 128
                dv = E("scalar", lambda e, row=row, idx=idx, b=b:
                       e.dma_start(out=sl_rs_in[b][row:row + 128, :],
                                   in_=slst(idx)),
                       waits=[("V", vv)], inc=ssem, amt=16)
                sl_dma[idx] = (ssem, dv)
                idx += 1
            slots_tK[b] = em.val("T")

            if b == 0:
                rs0 = E("gpsimd", lambda e: e.collective_compute(
                    "ReduceScatter", ALU.add, replica_groups=rg,
                    ins=[sl_rs_in[0]], outs=[sl_rs_out[0]]),
                    waits=[("slD0", em.val("slD0")),
                           ("slD1", em.val("slD1"))], inc="C")
        rs1 = E("gpsimd", lambda e: e.collective_compute(
            "ReduceScatter", ALU.add, replica_groups=rg,
            ins=[sl_rs_in[1]], outs=[sl_rs_out[1]]),
            waits=[("slD2", em.val("slD2")), ("slD3", em.val("slD3"))],
            inc="C")
        rs_done = {0: rs0, 1: rs1}

        ba.free("L1")
        ba.free("DIFF")
        ba.free("X1LN")
        ba.free("SLST")

        ba.alloc("SLBF", 2 * D)
        ba.alloc("SLT0", 8 * 256)
        ba.alloc("SLT1", 8 * 256)
        ba.alloc("ROWS2", 8 * D)
        ba.alloc("WBUF", 2 * 8 * D)
        ba.alloc("YST", 2 * D)

        def slbf(i):
            off, _ = ba.lookup("SLBF")
            return ABF[:, off + (i % 2) * D: off + (i % 2) * D + D]

        def rows2():
            off, _ = ba.lookup("ROWS2")
            return ABF[0:1, off:off + 8 * D]

        def wbuf(i):
            off, _ = ba.lookup("WBUF")
            i = i % 2
            return ABF[:, off + i * 8 * D: off + (i + 1) * 8 * D]

        def yst(i):
            off, _ = ba.lookup("YST")
            i = i % 2
            return ABF[:, off + i * D: off + (i + 1) * D]

        E("gpsimd", lambda e: e.dma_start(
            out=rows2(), in_=fb_in.rearrange("(o n) e -> o (n e)", o=1)),
            waits=[("T", slots_tK[0])], inc="wL", amt=16)
        fb_done = em.val("wL")
        wb_load = {}
        for k in (0, 1):
            wb_load[k] = (f"wb{k % 2}", E(
                "gpsimd", lambda e, k=k: e.dma_start(
                    out=wbuf(k).rearrange("p (c d) -> p c d", d=D),
                    in_=w_exp[k].rearrange("(c p) d -> p c d", p=128)),
                waits=[("T", slots_tK[0])] if k == 0 else (),
                inc=f"wb{k % 2}", amt=16))

        yT = {}
        y_dma = {}
        slt_done = {}
        prev_slt = ("tpX", 0)
        for b in (0, 1):
            sltn = f"SLT{b}"
            for i in range(2):
                E("sync", lambda e, i=i, b=b: e.dma_start(
                    out=slbf(i), in_=sl_rs_out[b][i * 128:(i + 1) * 128, :]),
                    waits=[("C", rs_done[b]), prev_slt,
                           ("slD2", em.val("slD2")),
                           ("slD3", em.val("slD3"))] if i == 0 else (),
                    inc="slL", amt=16)
            slL_all = em.val("slL")
            for i in range(2):
                E("sync", lambda e, i=i, sltn=sltn: e.dma_start(
                    out=bf(sltn).rearrange("p (c q) -> p c q", c=8)
                    [:, :, i * 128:(i + 1) * 128],
                    in_=slbf(i).rearrange("p (c q) -> p c q", c=8),
                    transpose=True),
                    waits=[("slL", slL_all)] if i == 0 else (),
                    inc="tpX", amt=16)
            slt_done[b] = ("tpX", em.val("tpX"))
            prev_slt = slt_done[b]

        for n in range(NEXP):
            for bb in (0, 1):
                sltn = f"SLT{bb}"
                psn = 2 * (n % 2) + bb
                g = ps_guard[psn]
                for ns in range(2):
                    E("tensor", lambda e, n=n, ns=ns, psn=psn: e.matmul(
                        PS[psn][0:32, ns * 512:(ns + 1) * 512],
                        ones()[:, 0:32],
                        rows2()[:, n * D + ns * 512: n * D + (ns + 1) * 512],
                        start=True, stop=False),
                        waits=[g, wb_load[n], slt_done[bb], ("wL", fb_done)]
                        if ns == 0 else ())
                    for c in range(8):
                        E("tensor", lambda e, n=n, ns=ns, c=c, psn=psn,
                          sltn=sltn: e.matmul(
                            PS[psn][0:32, ns * 512:(ns + 1) * 512],
                            bf(sltn)[:, c * 256 + n * 32:
                                     c * 256 + (n + 1) * 32],
                            wbuf(n)[:, c * D + ns * 512:
                                    c * D + (ns + 1) * 512],
                            start=False, stop=(c == 7)))
            yT[n] = em.tag_last("tensor", "T")
            if n + 2 < NEXP:
                wb_load[n + 2] = (f"wb{n % 2}", E(
                    "gpsimd", lambda e, n=n: e.dma_start(
                        out=wbuf(n + 2).rearrange("p (c d) -> p c d", d=D),
                        in_=w_exp[n + 2].rearrange("(c p) d -> p c d", p=128)),
                    waits=[("T", yT[n])], inc=f"wb{n % 2}", amt=16))
            ysem = f"yD{n % 2}"
            pa = 2 * (n % 2)
            v0 = E("scalar", lambda e, n=n, pa=pa: e.activation(
                yst(n)[0:32, :], PS[pa][0:32, :], AF.Identity, bias=0.0,
                scale=1.0),
                waits=[("T", yT[n])] +
                ([y_dma[n - 2]] if n >= 2 else []), inc="S")
            v1 = E("scalar", lambda e, n=n, pa=pa: e.activation(
                yst(n)[32:64, :], PS[pa + 1][0:32, :], AF.Identity, bias=0.0,
                scale=1.0), waits=[("S", v0)], inc="S")
            ps_guard[pa] = ("S", v1)
            ps_guard[pa + 1] = ("S", v1)
            for bb in (0, 1):
                dv = E("scalar", lambda e, n=n, bb=bb: e.dma_start(
                    out=y_ag_in[bb * 256 + n * 32: bb * 256 + (n + 1) * 32, :],
                    in_=yst(n)[bb * 32:(bb + 1) * 32, :]),
                    waits=[("S", v1)] if bb == 0 else (), inc=ysem, amt=16)
            y_dma[n] = (ysem, dv)

        ag_done = E("gpsimd", lambda e: e.collective_compute(
            "AllGather", ALU.bypass, replica_groups=rg,
            ins=[y_ag_in], outs=[y_ag_out]),
            waits=[("yD0", em.val("yD0")), ("yD1", em.val("yD1"))],
            inc="C")

        ba.alloc("YB", 16 * D)
        ba.alloc("CT", 16 * D)
        fa.alloc("X1R", 2 * D)
        fa.alloc("OST", 2 * D)
        fa.alloc("OSNK", D)

        def x1r(i):
            off, _ = fa.lookup("X1R")
            i = i % 2
            return AF32[:, off + i * D: off + (i + 1) * D]

        def ost(i):
            off, _ = fa.lookup("OST")
            i = i % 2
            return AF32[:, off + i * D: off + (i + 1) * D]

        out_dma = {}
        xr_done = {}
        v7_done = {}
        prev_T = {0: slots_tK[1], 1: 0}
        for b in (0, 1):
            tb = b * NT
            for j in range(16):
                row = (j // 2) * 512 + b * 256 + (j % 2) * 128
                E("gpsimd", lambda e, j=j, row=row: e.dma_start(
                    out=bf("YB")[:, j * D:(j + 1) * D],
                    in_=y_ag_out[row:row + 128, :]),
                    waits=[("C", ag_done), ("T", prev_T[b]),
                           ("cD", comb_dma_done)]
                    if j == 0 else (), inc="yB", amt=16)
            for j in range(16):
                E("sync", lambda e, j=j, b=b: e.dma_start(
                    out=bf("CT")[:, j * D:(j + 1) * D],
                    in_=comb_d[b * TPB:(b + 1) * TPB, j * 128:(j + 1) * 128],
                    transpose=True),
                    waits=[("cD", comb_dma_done), ("T", prev_T[b])]
                    if j == 0 else (), inc="tpC", amt=16)
            yb_done = ("yB", em.val("yB"))
            ct_done = ("tpC", em.val("tpC"))

            for t in range(NT):
                gt = tb + t
                psn = t % 4
                g = ps_guard[psn]
                xr_done[gt] = (f"x1R{gt % 2}", E(
                    "gpsimd", lambda e, gt=gt: e.dma_start(
                        out=x1r(gt), in_=x1_d[gt * 128:(gt + 1) * 128, :]),
                    waits=[("spD", x1_spill[gt])] +
                    ([("V", v7_done[gt - 2])] if gt >= 2 else []),
                    inc=f"x1R{gt % 2}", amt=16))
                for j in range(16):
                    lhs = bf("CT")[:, j * D + t * 128: j * D + t * 128 + 128]
                    for n in range(2):
                        E("tensor", lambda e, lhs=lhs, psn=psn, j=j, n=n:
                          e.matmul(
                            PS[psn][:, n * 512:(n + 1) * 512], lhs,
                            bf("YB")[:, j * D + n * 512: j * D + (n + 1) * 512],
                            start=(j == 0), stop=(j == 15)),
                            waits=[g, yb_done, ct_done]
                            if (j == 0 and n == 0) else ())
                tK = em.tag_last("tensor", "T")
                ev = scr(t % 2)
                vE = E("vector", lambda e, ev=ev, psn=psn: e.tensor_copy(
                    ev, PS[psn][:, :]),
                    waits=[("T", tK)] +
                    ([("V", v7_done[gt - 2])] if gt >= 2 else []), inc="V")
                ps_guard[psn] = ("V", vE)
                s2 = E("scalar", lambda e, ev=ev: e.activation(
                    bf("SINK"), ev, AF.Square, bias=0.0,
                    scale=1.0, accum_out=tmp8[:, 5:6]),
                    waits=[("V", vE)], inc="S")
                v2 = E("vector", lambda e, ev=ev: e.reduce_sum(
                    tmp8[:, 4:5], ev, axis=AX.X, negate=True),
                    waits=[("V", vE)], inc="V")
                v3 = E("vector", lambda e: e.tensor_scalar_mul(
                    tmp8[:, 4:5], tmp8[:, 4:5], INV_D),
                    waits=[("V", v2)], inc="V")
                v3b = E("vector", lambda e: e.tensor_mul(
                    tmp8[:, 9:10], tmp8[:, 4:5], tmp8[:, 4:5]),
                    waits=[("V", v3)], inc="V")
                v4 = E("vector", lambda e: e.tensor_scalar_mul(
                    tmp8[:, 5:6], tmp8[:, 5:6], INV_D),
                    waits=[("S", s2), ("V", v3b)], inc="V")
                v4b = E("vector", lambda e: e.tensor_sub(
                    tmp8[:, 5:6], tmp8[:, 5:6], tmp8[:, 9:10]),
                    waits=[("V", v4)], inc="V")
                s3 = E("scalar", lambda e: e.activation(
                    tmp8[:, 5:6], tmp8[:, 5:6], AF.Sqrt, bias=eps_col,
                    scale=1.0), waits=[("V", v4b)], inc="S")
                v5 = E("vector", lambda e: e.reciprocal(
                    tmp8[:, 5:6], tmp8[:, 5:6]), waits=[("S", s3)], inc="V")
                v6 = E("vector", lambda e: e.tensor_mul(
                    tmp8[:, 4:5], tmp8[:, 4:5], tmp8[:, 5:6]),
                    waits=[("V", v5)], inc="V")
                sX = E("scalar", lambda e, gt=gt, t=t: e.activation(
                    scr(2 + t % 2), x1r(gt), AF.Identity,
                    bias=tmp8[:, 4:5], scale=1.0),
                    waits=[("V", v6), xr_done[gt],
                           ("spD", x1_spill[2 * NT - 1])], inc="S")
                v7 = E("vector", lambda e, gt=gt, t=t, ev=ev:
                       e.scalar_tensor_tensor(
                        ost(gt), ev, tmp8[:, 5:6], scr(2 + t % 2),
                        ALU.mult, ALU.add),
                    waits=[("S", sX)] +
                    ([(f"O{gt % 2}", out_dma[gt - 2])] if gt >= 2 else []),
                    inc="V")
                v7_done[gt] = v7
                out_dma[gt] = E("scalar", lambda e, gt=gt: e.dma_start(
                    out=out_d[gt * 128:(gt + 1) * 128, :], in_=ost(gt)),
                    waits=[("V", v7)], inc=f"O{gt % 2}", amt=16)
            prev_T[1] = em.val("T")

        final_o = [("O0", em.val("O0")), ("O1", em.val("O1"))]
        em.wait("sync", final_o)
        em.wait("gpsimd", final_o)

        @block.sync
        def _(eng):
            for fn in em.ops["sync"]:
                fn(eng, sems)

        @block.scalar
        def _(eng):
            for fn in em.ops["scalar"]:
                fn(eng, sems)

        @block.vector
        def _(eng):
            for fn in em.ops["vector"]:
                fn(eng, sems)

        @block.tensor
        def _(eng):
            for fn in em.ops["tensor"]:
                fn(eng, sems)

        @block.gpsimd
        def _(eng):
            for fn in em.ops["gpsimd"]:
                fn(eng, sems)

    return nc


_CACHE = {}


def kernel(L0_attn_x, L0_attn_norm_w, L0_attn_norm_b, L0_attn_qkv_w,
           L0_attn_qkv_b, L0_attn_proj_w, L0_attn_proj_b, L0_ffn_phi,
           L0_ffn_W, L0_ffn_b):
    x = np.asarray(L0_attn_x, dtype=np.float32)
    qkv_wT = np.ascontiguousarray(np.asarray(L0_attn_qkv_w, np.float32).T)
    qkv_b_ = np.asarray(L0_attn_qkv_b, np.float32).reshape(1, 3 * D)
    proj_wT = np.ascontiguousarray(np.asarray(L0_attn_proj_w, np.float32).T)
    proj_b_ = np.asarray(L0_attn_proj_b, np.float32).reshape(1, D)
    norm_w_ = np.asarray(L0_attn_norm_w, np.float32).reshape(1, D)
    norm_b_ = np.asarray(L0_attn_norm_b, np.float32).reshape(1, D)
    phi2_ = np.ascontiguousarray(
        np.asarray(L0_ffn_phi, np.float32).reshape(D, NP))
    W = np.asarray(L0_ffn_W, np.float32)
    fbv = np.asarray(L0_ffn_b, np.float32)

    if "nc" not in _CACHE:
        _CACHE["nc"] = build_nc()
    nc = _CACHE["nc"]

    in_maps = []
    for c in range(NCORES):
        xc = np.ascontiguousarray(
            x[:, c * TPB:(c + 1) * TPB, :].reshape(2 * TPB, D))
        in_maps.append({
            "x_in": xc, "qkv_wT": qkv_wT, "qkv_b": qkv_b_,
            "proj_wT": proj_wT, "proj_b": proj_b_,
            "norm_w": norm_w_, "norm_b": norm_b_, "phi2": phi2_,
            "w_exp": np.ascontiguousarray(W[c * NEXP:(c + 1) * NEXP]),
            "fb_in": np.ascontiguousarray(fbv[c * NEXP:(c + 1) * NEXP]),
        })

    res = run_bass_kernel_spmd(nc, in_maps, core_ids=list(range(NCORES)))

    out = np.empty((2, 8192, D), np.float32)
    for c in range(NCORES):
        r = np.asarray(res.results[c]["out"])
        out[0, c * TPB:(c + 1) * TPB] = r[:TPB]
        out[1, c * TPB:(c + 1) * TPB] = r[TPB:]
    return out


# version 21
# speedup vs baseline: 1.0179x; 1.0179x over previous
import numpy as np
from contextlib import ExitStack

import concourse.bass as bass
import concourse.mybir as mybir
from concourse.bass_utils import run_bass_kernel_spmd

F32 = mybir.dt.float32
BF16 = mybir.dt.bfloat16
AF = mybir.ActivationFunctionType
ALU = mybir.AluOpType
AX = mybir.AxisListType

NCORES = 8
D = 1024
TPB = 1024
NT = 8
NP = 2048
NEXP = 8
EPS = 1e-5
INV_D = 1.0 / D
ATT_SCALE = 1.0 / 32.0

ENGINES = ("sync", "scalar", "vector", "tensor", "gpsimd")


class Arena:
    def __init__(self, cols):
        self.cols = cols
        self.free_list = [(0, cols)]
        self.allocs = {}
        self.dead = {}

    def lookup(self, name):
        return self.allocs.get(name) or self.dead[name]

    def alloc(self, name, n, align=64, side="left"):
        assert name not in self.allocs, name
        blocks = enumerate(self.free_list)
        if side == "right":
            blocks = reversed(list(blocks))
        for i, (off, sz) in blocks:
            if side == "right":
                aoff = (off + sz - n) // align * align
                if aoff < off:
                    continue
            else:
                aoff = (off + align - 1) // align * align
                if aoff + n > off + sz:
                    continue
            repl = [p for p in ((off, aoff - off),
                                (aoff + n, off + sz - (aoff + n)))
                    if p[1] > 0]
            self.free_list[i:i + 1] = repl
            self.allocs[name] = (aoff, n)
            return aoff
        raise MemoryError(f"arena alloc failed: {name} ({n} cols); "
                          f"free={self.free_list}")

    def free(self, name):
        off, n = self.allocs.pop(name)
        self.dead[name] = (off, n)
        self.free_list.append((off, n))
        self.free_list.sort()
        merged = []
        for o, s in self.free_list:
            if merged and merged[-1][0] + merged[-1][1] == o:
                merged[-1] = (merged[-1][0], merged[-1][1] + s)
            else:
                merged.append((o, s))
        self.free_list = merged


class Emitter:
    def __init__(self):
        self.ops = {e: [] for e in ENGINES}
        self.count = {}
        self.last = {}

    def val(self, sem):
        return self.count.get(sem, 0)

    def emit(self, eng, fn, waits=(), inc=None, amt=1):
        waits = tuple((s, v) for s, v in waits if v > 0)
        newval = None
        if inc is not None:
            newval = self.count.get(inc, 0) + amt
            self.count[inc] = newval

        def runner(e, sems, fn=fn, waits=waits, inc=inc, amt=amt, eng=eng):
            for s, v in waits:
                e.wait_ge(sems[s], v)
            ins = fn(e)
            self.last[eng] = ins
            if inc is not None:
                ins.then_inc(sems[inc], amt)

        self.ops[eng].append(runner)
        return newval

    def tag_last(self, eng, inc, amt=1):
        newval = self.count.get(inc, 0) + amt
        self.count[inc] = newval

        def runner(e, sems, inc=inc, amt=amt, eng=eng):
            self.last[eng].then_inc(sems[inc], amt)
        self.ops[eng].append(runner)
        return newval

    def wait(self, eng, waits):
        waits = tuple((s, v) for s, v in waits if v > 0)

        def runner(e, sems, waits=waits):
            for s, v in waits:
                e.wait_ge(sems[s], v)
        self.ops[eng].append(runner)


def build_nc():
    nc = bass.Bass("TRN2", target_bir_lowering=False, debug=False)

    x_in = nc.dram_tensor("x_in", [2 * TPB, D], F32, kind="ExternalInput").ap()
    qkv_wT = nc.dram_tensor("qkv_wT", [D, 3 * D], F32, kind="ExternalInput").ap()
    qkv_b = nc.dram_tensor("qkv_b", [1, 3 * D], F32, kind="ExternalInput").ap()
    proj_wT = nc.dram_tensor("proj_wT", [D, D], F32, kind="ExternalInput").ap()
    proj_b = nc.dram_tensor("proj_b", [1, D], F32, kind="ExternalInput").ap()
    norm_w = nc.dram_tensor("norm_w", [1, D], F32, kind="ExternalInput").ap()
    norm_b = nc.dram_tensor("norm_b", [1, D], F32, kind="ExternalInput").ap()
    phi2 = nc.dram_tensor("phi2", [D, NP], F32, kind="ExternalInput").ap()
    w_exp = nc.dram_tensor("w_exp", [NEXP, D, D], F32, kind="ExternalInput").ap()
    fb_in = nc.dram_tensor("fb_in", [NEXP, D], F32, kind="ExternalInput").ap()

    out_d = nc.dram_tensor("out", [2 * TPB, D], F32, kind="ExternalOutput").ap()

    kv_ar_in = nc.dram_tensor("kv_ar_in", [2 * D, D], BF16).ap()
    kv_ar_out = nc.dram_tensor("kv_ar_out", [2 * D, D], BF16,
                               addr_space="Shared").ap()
    sl_rs_in = [nc.dram_tensor(f"sl_rs_in{b}", [NP, D], BF16).ap()
                for b in (0, 1)]
    sl_rs_out = [nc.dram_tensor(f"sl_rs_out{b}", [NP // NCORES, D], BF16).ap()
                 for b in (0, 1)]
    y_ag_in = nc.dram_tensor("y_ag_in", [512, D], BF16).ap()
    y_ag_out = nc.dram_tensor("y_ag_out", [2 * NP, D], BF16,
                              addr_space="Shared").ap()
    comb_d = nc.dram_tensor("comb_d", [2 * TPB, NP], BF16).ap()
    x1_d = nc.dram_tensor("x1_d", [2 * TPB, D], F32).ap()

    rg = [list(range(NCORES))]

    em = Emitter()
    fa = Arena(14336)
    ba = Arena(77440)

    with ExitStack() as ctx:
        AF32 = ctx.enter_context(nc.sbuf_tensor("AF32", [128, fa.cols], F32))
        ABF = ctx.enter_context(nc.sbuf_tensor("ABF", [128, ba.cols], BF16))
        PS = [ctx.enter_context(nc.psum_tensor(f"PS{i}", [128, 1024], F32))
              for i in range(4)]
        sems = {}
        for s in ("xL0", "xL1", "xL2", "xL3", "tpH00", "tpH01", "tpH10", "tpH11", "tpX",
                  "wL", "wb0", "wb1", "kvD00", "kvD01", "kvD10", "kvD11", "spD", "cD",
                  "slD0", "slD1", "slD2", "slD3", "yD0", "yD1", "yD2", "yD3", "kvL0", "kvL1",
                  "x1R0", "x1R1", "yB", "yB1", "yB2", "yB3", "slL",
                  "tpC", "tpC1", "tpC2", "tpC3", "O0", "O1",
                  "T", "V", "S", "G", "C"):
            sems[s] = ctx.enter_context(nc.semaphore(s))
        block = ctx.enter_context(nc.Block())

        def f32(name):
            off, n = fa.lookup(name)
            return AF32[:, off:off + n]

        def bf(name):
            off, n = ba.lookup(name)
            return ABF[:, off:off + n]

        E = em.emit

        fa.alloc("COLS", 256)
        fa.alloc("ROWSF", 2048)
        fa.alloc("ONESF", 128)
        _colreg = {"next": 0, "map": {}}

        def cols(name, n):
            if name not in _colreg["map"]:
                _colreg["map"][name] = (_colreg["next"], n)
                _colreg["next"] += n
                assert _colreg["next"] <= 256
            off, nn = _colreg["map"][name]
            assert nn == n
            c0, _ = fa.lookup("COLS")
            return AF32[:, c0 + off:c0 + off + n]

        def rowsf(a, b_):
            off, _ = fa.lookup("ROWSF")
            return AF32[0:1, off + a:off + b_]

        def onesf():
            off, _ = fa.lookup("ONESF")
            return AF32[0:1, off:off + 128]

        ba.alloc("ONES", 128)
        ba.alloc("SINK", D)
        ba.alloc("ROWS", 4096)

        def ones():
            off, _ = ba.lookup("ONES")
            return ABF[0:1, off:off + 128]

        def rows(a, b_):
            off, _ = ba.lookup("ROWS")
            return ABF[0:1, off + a:off + b_]

        ba.alloc("X1LN", 16 * D, side="right")
        o_T = {}
        ba.alloc("QW_KV", 8 * 2 * D)

        E("gpsimd", lambda e: e.memset(ones(), 1.0), inc="G")
        E("gpsimd", lambda e: e.memset(onesf(), 1.0), inc="G")
        eps_col = cols("eps", 1)
        E("gpsimd", lambda e: e.memset(eps_col, EPS), inc="G")
        eps2_col = cols("eps2", 1)
        E("gpsimd", lambda e: e.memset(eps2_col, EPS * EPS), inc="G")
        ones_done = em.val("G")

        E("gpsimd", lambda e: e.dma_start(out=rows(0, 3 * D), in_=qkv_b),
          inc="wL", amt=16)
        E("gpsimd", lambda e: e.dma_start(out=rows(3 * D, 4 * D), in_=proj_b),
          inc="wL", amt=16)
        E("gpsimd", lambda e: e.dma_start(out=rowsf(0, D), in_=norm_w),
          inc="wL", amt=16)
        E("gpsimd", lambda e: e.dma_start(out=rowsf(D, 2 * D), in_=norm_b),
          inc="wL", amt=16)
        qb_col = cols("qb", 8)
        E("gpsimd", lambda e: e.dma_start(
            out=qb_col, in_=qkv_b.rearrange("1 (j p) -> p j", p=128)[:, 0:8],
            allow_slow_non_contiguous=True), inc="wL", amt=16)
        rows_done = em.val("wL")

        qw_chunk = {}
        for c in range(8):
            qw_chunk[c] = E("gpsimd", lambda e, c=c: e.dma_start(
                out=bf("QW_KV")[:, c * 2 * D:(c + 1) * 2 * D],
                in_=qkv_wT[c * 128:(c + 1) * 128, D:3 * D]),
                waits=[("wL", qw_chunk[c - 1])] if c > 0 else
                [("wL", rows_done)],
                inc="wL", amt=16)
        qw_done = em.val("wL")

        fa.alloc("SCR", 4 * D)
        fa.alloc("WB", D)
        fa.alloc("BB", D)
        E("tensor", lambda e: e.matmul(
            PS[0][:, 0:512], onesf(), rowsf(0, 512), start=True, stop=True),
            waits=[("wL", rows_done), ("G", ones_done)])
        E("tensor", lambda e: e.matmul(
            PS[0][:, 512:1024], onesf(), rowsf(512, D), start=True, stop=True))
        E("tensor", lambda e: e.matmul(
            PS[1][:, 0:512], onesf(), rowsf(D, D + 512), start=True,
            stop=True))
        tv = E("tensor", lambda e: e.matmul(
            PS[1][:, 512:1024], onesf(), rowsf(D + 512, 2 * D), start=True,
            stop=True), inc="T")
        E("vector", lambda e: e.tensor_copy(f32("WB"), PS[0][:, :]),
          waits=[("T", tv)], inc="V")
        bcast_done = E("vector", lambda e: e.tensor_copy(
            f32("BB"), PS[1][:, :]), inc="V")
        fa.alloc("KBB", D)
        fa.alloc("VBB", D)
        E("tensor", lambda e: e.matmul(
            PS[2][:, 0:512], ones(), rows(D, D + 512), start=True, stop=True))
        E("tensor", lambda e: e.matmul(
            PS[2][:, 512:1024], ones(), rows(D + 512, 2 * D), start=True,
            stop=True))
        E("tensor", lambda e: e.matmul(
            PS[3][:, 0:512], ones(), rows(2 * D, 2 * D + 512), start=True,
            stop=True))
        tv2 = E("tensor", lambda e: e.matmul(
            PS[3][:, 512:1024], ones(), rows(2 * D + 512, 3 * D), start=True,
            stop=True), inc="T")
        E("vector", lambda e: e.tensor_copy(f32("KBB"), PS[2][:, :]),
          waits=[("T", tv2)], inc="V")
        kvb_done = E("vector", lambda e: e.tensor_copy(
            f32("VBB"), PS[3][:, :]), inc="V")

        ps_guard = [("V", bcast_done), ("V", bcast_done),
                    ("V", kvb_done), ("V", kvb_done)]

        fa.alloc("XSTG", 3 * D)
        ba.alloc("KVST", 2 * D)

        xstg_guard = [("V", 0)] * 3

        def xstg(i):
            off, _ = fa.lookup("XSTG")
            i = i % 3
            return AF32[:, off + i * D: off + (i + 1) * D]

        def scr(i):
            off, _ = fa.lookup("SCR")
            i = i % 4
            return AF32[:, off + i * D: off + (i + 1) * D]

        def kvst(i):
            off, _ = ba.lookup("KVST")
            i = i % 2
            return ABF[:, off + i * D: off + (i + 1) * D]

        def load_x(slot, gt, extra=()):
            w = [xstg_guard[slot % 3]] + list(extra)
            sem = f"xL{slot % 4}"
            v = E("scalar", lambda e, slot=slot, gt=gt: e.dma_start(
                out=xstg(slot), in_=x_in[gt * 128:(gt + 1) * 128, :]),
                waits=w, inc=sem, amt=16)
            return (sem, v)

        rstd0 = cols("rstd0", 16)
        nmr0 = cols("nmr0", 16)
        nm0 = cols("nm0", 16)
        ssq0 = cols("ssq0", 16)
        var0 = cols("var0", 16)
        lncol = cols("lncol", 32)
        tmp8 = cols("tmp8", 16)
        smax = cols("smax", 8)

        qkv_ready = {}
        kvst_guard = [("kvD00", 0), ("kvD01", 0)]
        ar_done = {}
        kv_loaded = {}
        x1_spill = {}
        x1ln_done_tile = {}
        h_tp = {}

        s_sq = 0
        g_par = {0: 0, 1: 0}
        v_mul = {}
        for b in (0, 1):
            tb = b * NT
            a_off = 0 + b * 8
            c_off = 16 + b * 8
            ba.alloc(f"HT{b}", NT * D)
            ba.alloc(f"HST{b}", 2 * D)

            def hstage(i, b=b):
                off, _ = ba.lookup(f"HST{b}")
                i = i % 2
                return ABF[:, off + i * D: off + (i + 1) * D]

            for t in range(NT):
                gt = tb + t
                xl = load_x(gt, gt)
                s_sq = E("scalar", lambda e, gt=gt: e.activation(
                    bf("SINK"), xstg(gt), AF.Square, bias=0.0,
                    scale=1.0, accum_out=ssq0[:, gt:gt + 1]),
                    waits=[xl] + ([("S", s_sq)] if s_sq else []),
                    inc="S")
                v1 = E("vector", lambda e, gt=gt: e.reduce_sum(
                    nm0[:, gt:gt + 1], xstg(gt), axis=AX.X, negate=True),
                    waits=[xl], inc="V")
                v2 = E("vector", lambda e, gt=gt: e.tensor_scalar_mul(
                    nm0[:, gt:gt + 1], nm0[:, gt:gt + 1], INV_D),
                    waits=[("V", v1)], inc="V")
                v2b = E("vector", lambda e, gt=gt: e.tensor_mul(
                    tmp8[:, 8:9], nm0[:, gt:gt + 1], nm0[:, gt:gt + 1]),
                    waits=[("V", v2)], inc="V")
                v3 = E("vector", lambda e, gt=gt: e.tensor_scalar_mul(
                    var0[:, gt:gt + 1], ssq0[:, gt:gt + 1], INV_D),
                    waits=[("S", s_sq), ("V", v2b)], inc="V")
                v3 = E("vector", lambda e, gt=gt: e.tensor_sub(
                    var0[:, gt:gt + 1], var0[:, gt:gt + 1], tmp8[:, 8:9]),
                    waits=[("V", v3)], inc="V")
                s2 = E("scalar", lambda e, gt=gt: e.activation(
                    tmp8[:, 6:7], var0[:, gt:gt + 1], AF.Sqrt,
                    bias=eps2_col, scale=1.0 + EPS),
                    waits=[("V", v3)], inc="S")
                v6 = E("vector", lambda e, gt=gt, t=t, a_off=a_off:
                       e.reciprocal(
                        lncol[:, a_off + t:a_off + t + 1], tmp8[:, 6:7]),
                    waits=[("S", s2)], inc="V")
                v7 = E("vector", lambda e, gt=gt, t=t, a_off=a_off,
                       c_off=c_off: e.tensor_mul(
                        lncol[:, c_off + t:c_off + t + 1],
                        nm0[:, gt:gt + 1],
                        lncol[:, a_off + t:a_off + t + 1]),
                    waits=[("V", v6)], inc="V")
                s2b = E("scalar", lambda e, gt=gt: e.activation(
                    tmp8[:, 7:8], var0[:, gt:gt + 1], AF.Sqrt,
                    bias=eps_col, scale=1.0), waits=[("V", v3)], inc="S")
                v4 = E("vector", lambda e, gt=gt: e.reciprocal(
                    rstd0[:, gt:gt + 1], tmp8[:, 7:8]),
                    waits=[("S", s2b)], inc="V")
                v5 = E("vector", lambda e, gt=gt: e.tensor_mul(
                    nmr0[:, gt:gt + 1], nm0[:, gt:gt + 1],
                    rstd0[:, gt:gt + 1]), waits=[("V", v4)], inc="V")
                iA = gt % 2
                iB = 2 + gt % 2
                s4 = E("scalar", lambda e, gt=gt, t=t, iA=iA, a_off=a_off,
                       c_off=c_off: e.activation(
                           scr(iA), xstg(gt), AF.Identity,
                           bias=lncol[:, c_off + t:c_off + t + 1],
                           scale=lncol[:, a_off + t:a_off + t + 1]),
                       waits=[("V", v7)] +
                       ([("V", v_mul[gt - 2])] if gt >= 2 else []), inc="S")
                xstg_guard[gt % 3] = ("S", s4)
                v11 = E("vector", lambda e, iA=iA, iB=iB: e.tensor_mul(
                    scr(iB), scr(iA), f32("WB")),
                    waits=[("S", s4), ("V", bcast_done)] +
                    ([("G", g_par[gt % 2])] if g_par[gt % 2] else []),
                    inc="V")
                v_mul[gt] = v11
                g1 = E("gpsimd", lambda e, t=t, iB=iB, hstage=hstage:
                       e.tensor_add(hstage(t), scr(iB), f32("BB")),
                    waits=[("V", v11), ("G", ones_done)] +
                    ([h_tp[(b, t - 2)]] if t >= 2 else []), inc="G")
                g_par[gt % 2] = g1
                tsem = f"tpH{b}{t % 2}"
                E("sync", lambda e, t=t, b=b, hstage=hstage: e.dma_start(
                    out=bf(f"HT{b}").rearrange("p (c q) -> p c q", c=8)
                    [:, :, t * 128:(t + 1) * 128],
                    in_=hstage(t).rearrange("p (c q) -> p c q", c=8),
                    transpose=True),
                    waits=[("G", g1)], inc=tsem, amt=16)
                h_tp[(b, t)] = (tsem, em.val(tsem))

        for b in (0, 1):
            tb = b * NT
            ba.alloc(f"K{b}", NT * D)
            ba.alloc(f"V{b}", NT * D)
            kv_evac_v = 0
            kv_evac_s = 0
            for t in range(NT):
                g0, g1_ = ps_guard[0], ps_guard[1]
                first = True
                for c in range(8):
                    lhs = bf(f"HT{b}")[:, c * D + t * 128: c * D + t * 128 + 128]
                    for psn, fb0 in ((0, 0), (1, D)):
                        for n in range(2):
                            if first:
                                w = [h_tp[(b, t)], ("wL", rows_done),
                                     ("V", kvb_done), g0, g1_]
                                if b == 0 and t == 0:
                                    w.append(("wL", qw_chunk[c]))
                                first = False
                            elif (b == 0 and t == 0 and psn == 0
                                  and n == 0):
                                w = [("wL", qw_chunk[c])]
                            else:
                                w = ()
                            E("tensor", lambda e, lhs=lhs, psn=psn, c=c,
                              fb0=fb0, n=n: e.matmul(
                                PS[psn][:, n * 512:(n + 1) * 512], lhs,
                                bf("QW_KV")[:, c * 2 * D + fb0 + n * 512:
                                            c * 2 * D + fb0 + (n + 1) * 512],
                                start=(c == 0), stop=(c == 7)),
                                waits=w)
                tK = em.tag_last("tensor", "T")
                kv_evac_v = E("vector", lambda e, t=t, b=b: e.tensor_add(
                    bf(f"K{b}")[:, t * D:(t + 1) * D], PS[0][:, :],
                    f32("KBB")),
                    waits=[("T", tK)], inc="V")
                kv_evac_s = E("vector", lambda e, t=t, b=b: e.tensor_add(
                    bf(f"V{b}")[:, t * D:(t + 1) * D], PS[1][:, :],
                    f32("VBB")),
                    waits=[("V", kv_evac_v)], inc="V")
                ps_guard[0] = ("V", kv_evac_v)
                ps_guard[1] = ("V", kv_evac_s)

            kv_dma = {}
            for c1 in range(8):
                psn = 2 + (c1 % 2)
                g = ps_guard[psn]
                for n in range(2):
                    for t in range(NT):
                        E("tensor", lambda e, c1=c1, n=n, t=t, psn=psn, b=b:
                          e.matmul(
                            PS[psn][:, n * 512:(n + 1) * 512],
                            bf(f"K{b}")[:, t * D + c1 * 128:
                                        t * D + c1 * 128 + 128],
                            bf(f"V{b}")[:, t * D + n * 512:
                                        t * D + (n + 1) * 512],
                            start=(t == 0), stop=(t == NT - 1)),
                            waits=[("V", kv_evac_s), g]
                            if (n == 0 and t == 0) else ())
                tK = em.tag_last("tensor", "T")
                vv = E("vector", lambda e, c1=c1, psn=psn: e.tensor_copy(
                    kvst(c1), PS[psn][:, :]),
                    waits=[("T", tK), kvst_guard[c1 % 2]], inc="V")
                ps_guard[psn] = ("V", vv)
                ksem = f"kvD{b}{c1 % 2}"
                kv_dma[c1] = E("scalar", lambda e, c1=c1, b=b: e.dma_start(
                    out=kv_ar_in[b * D + c1 * 128: b * D + (c1 + 1) * 128, :],
                    in_=kvst(c1)), waits=[("V", vv)], inc=ksem, amt=16)
                kvst_guard[c1 % 2] = (ksem, kv_dma[c1])

            ar_done[b] = E("gpsimd", lambda e, b=b: e.collective_compute(
                "AllReduce", ALU.add, replica_groups=rg,
                ins=[kv_ar_in[b * D:(b + 1) * D, :]],
                outs=[kv_ar_out[b * D:(b + 1) * D, :]]),
                waits=[(f"kvD{b}0", em.val(f"kvD{b}0")),
                       (f"kvD{b}1", em.val(f"kvD{b}1"))],
                inc="C")

            ba.free(f"HST{b}")
            ba.free(f"K{b}")
            ba.free(f"V{b}")

        ba.free("QW_KV")
        kvpart_T = em.val("T")
        ba.alloc("QW_Q", 8 * D)
        for c in range(8):
            E("gpsimd", lambda e, c=c: e.dma_start(
                out=bf("QW_Q")[:, c * D:(c + 1) * D],
                in_=qkv_wT[c * 128:(c + 1) * 128, 0:D]),
                waits=[("T", kvpart_T)] if c == 0 else (), inc="wL", amt=16)
        qwq_done = em.val("wL")
        for b in (0, 1):
            ba.alloc(f"QT{b}", NT * D)
            for f in range(8):
                psn = 2 + (f % 2)
                g = ps_guard[psn]
                for c in range(8):
                    lhs = bf("QW_Q")[:, c * D + f * 128: c * D + f * 128 + 128]
                    for n in range(2):
                        E("tensor", lambda e, lhs=lhs, psn=psn, c=c, n=n, b=b:
                          e.matmul(
                            PS[psn][:, n * 512:(n + 1) * 512], lhs,
                            bf(f"HT{b}")[:, c * D + n * 512:
                                         c * D + (n + 1) * 512],
                            start=(c == 0), stop=(c == 7)),
                            waits=[g, ("wL", qwq_done)]
                            if (c == 0 and n == 0) else ())
                tK = em.tag_last("tensor", "T")
                sv = E("scalar", lambda e, f=f, psn=psn, b=b: e.activation(
                    bf(f"QT{b}")[:, f * D:(f + 1) * D], PS[psn][:, :],
                    AF.Identity, bias=qb_col[:, f:f + 1], scale=1.0),
                    waits=[("T", tK)], inc="S")
                ps_guard[psn] = ("S", sv)
            qkv_ready[b] = em.val("S")
        ba.free("HT0")
        ba.free("HT1")
        ba.free("QW_Q")
        q_T = em.val("T")

        for b in (0, 1):
            ba.alloc(f"KV{b}", 8 * D)
            kv_loaded[b] = (f"kvL{b}", E(
                "gpsimd", lambda e, b=b: e.dma_start(
                    out=bf(f"KV{b}").rearrange("p (c d) -> p c d", d=D),
                    in_=kv_ar_out.rearrange(
                        "(b c p) d -> b p c d", b=2, p=128)[b]),
                waits=[("C", ar_done[b]), ("S", qkv_ready[b]), ("T", q_T)],
                inc=f"kvL{b}", amt=16))

        ba.alloc("PW", 8 * D)
        for c in range(8):
            E("gpsimd", lambda e, c=c: e.dma_start(
                out=bf("PW")[:, c * D:(c + 1) * D],
                in_=proj_wT[c * 128:(c + 1) * 128, :]),
                waits=[("T", q_T)] if c == 0 else (),
                inc="wL", amt=16)
        pw_done = em.val("wL")

        for b in (0, 1):
            tb = b * NT
            ba.alloc(f"OT{b}", NT * D)
            for c2 in range(8):
                psn = c2 % 2
                g = ps_guard[psn]
                for c1 in range(8):
                    lhs = bf(f"KV{b}")[:, c1 * D + c2 * 128:
                                       c1 * D + c2 * 128 + 128]
                    for n in range(2):
                        E("tensor", lambda e, lhs=lhs, psn=psn, c1=c1, n=n,
                          b=b: e.matmul(
                            PS[psn][:, n * 512:(n + 1) * 512], lhs,
                            bf(f"QT{b}")[:, c1 * D + n * 512:
                                         c1 * D + (n + 1) * 512],
                            start=(c1 == 0), stop=(c1 == 7)),
                            waits=[kv_loaded[b], g]
                            if (c1 == 0 and n == 0) else ())
                tK = em.tag_last("tensor", "T")
                sv = E("scalar", lambda e, c2=c2, psn=psn, b=b: e.activation(
                    bf(f"OT{b}")[:, c2 * D:(c2 + 1) * D], PS[psn][:, :],
                    AF.Identity, bias=0.0, scale=ATT_SCALE),
                    waits=[("T", tK)], inc="S")
                ps_guard[psn] = ("S", sv)
            ot_ready = em.val("S")
            o_T[b] = em.val("T")

            for t in range(NT):
                gt = tb + t
                psn = 2 + (t % 2)
                g = ps_guard[psn]
                xl = load_x(gt, gt)
                E("tensor", lambda e, psn=psn: e.matmul(
                    PS[psn][:, 0:512], ones(), rows(3 * D, 3 * D + 512),
                    start=True, stop=False),
                    waits=[g, ("S", ot_ready), ("wL", pw_done)])
                E("tensor", lambda e, psn=psn: e.matmul(
                    PS[psn][:, 512:1024], ones(), rows(3 * D + 512, 4 * D),
                    start=True, stop=False))
                for c2 in range(8):
                    lhs = bf(f"OT{b}")[:, c2 * D + t * 128:
                                       c2 * D + t * 128 + 128]
                    for n in range(2):
                        E("tensor", lambda e, lhs=lhs, psn=psn, c2=c2, n=n:
                          e.matmul(
                            PS[psn][:, n * 512:(n + 1) * 512], lhs,
                            bf("PW")[:, c2 * D + n * 512:
                                     c2 * D + (n + 1) * 512],
                            start=False, stop=(c2 == 7)))
                tK = em.tag_last("tensor", "T")
                s1 = E("scalar", lambda e, gt=gt: e.activation(
                    scr(0), xstg(gt), AF.Identity, bias=nmr0[:, gt:gt + 1],
                    scale=rstd0[:, gt:gt + 1]),
                    waits=[("T", tK), xl], inc="S")
                v1 = E("vector", lambda e, psn=psn: e.tensor_add(
                    scr(1), PS[psn][:, :], scr(0)),
                    waits=[("S", s1), ("T", tK)], inc="V")
                ps_guard[psn] = ("V", v1)
                v2 = E("vector", lambda e: e.reduce_sum(
                    tmp8[:, 0:1], scr(1), axis=AX.X, negate=True),
                    waits=[("V", v1)], inc="V")
                v3 = E("vector", lambda e: e.tensor_scalar_mul(
                    tmp8[:, 0:1], tmp8[:, 0:1], INV_D),
                    waits=[("V", v2)], inc="V")
                s2 = E("scalar", lambda e: e.activation(
                    scr(0), scr(1), AF.Square, bias=tmp8[:, 0:1], scale=1.0,
                    accum_out=tmp8[:, 1:2]), waits=[("V", v3)], inc="S")
                v4 = E("vector", lambda e: e.tensor_scalar_mul(
                    tmp8[:, 1:2], tmp8[:, 1:2], INV_D),
                    waits=[("S", s2)], inc="V")
                s3 = E("scalar", lambda e: e.activation(
                    tmp8[:, 1:2], tmp8[:, 1:2], AF.Sqrt, bias=eps_col, scale=1.0),
                    waits=[("V", v4)], inc="S")
                v5 = E("vector", lambda e: e.reciprocal(
                    tmp8[:, 1:2], tmp8[:, 1:2]), waits=[("S", s3)], inc="V")
                v6 = E("vector", lambda e: e.tensor_mul(
                    tmp8[:, 0:1], tmp8[:, 0:1], tmp8[:, 1:2]),
                    waits=[("V", v5)], inc="V")
                s4 = E("scalar", lambda e: e.activation(
                    scr(2), scr(1), AF.Identity, bias=tmp8[:, 0:1],
                    scale=tmp8[:, 1:2]), waits=[("V", v6)], inc="S")
                v7 = E("gpsimd", lambda e, gt=gt: e.tensor_add(
                    scr(3), xstg(gt), scr(2)),
                    waits=[("S", s4)] +
                    ([("spD", x1_spill[gt - 1])] if gt >= 1 else []), inc="G")
                xstg_guard[gt % 3] = ("G", v7)
                x1_spill[gt] = E("scalar", lambda e, gt=gt: e.dma_start(
                    out=x1_d[gt * 128:(gt + 1) * 128, :], in_=scr(3)),
                    waits=[("G", v7)], inc="spD", amt=16)
                v8 = E("vector", lambda e: e.reduce_sum(
                    tmp8[:, 2:3], scr(3), axis=AX.X, negate=True),
                    waits=[("G", v7)], inc="V")
                v9 = E("vector", lambda e: e.tensor_scalar_mul(
                    tmp8[:, 2:3], tmp8[:, 2:3], INV_D),
                    waits=[("V", v8)], inc="V")
                s5 = E("scalar", lambda e: e.activation(
                    scr(0), scr(3), AF.Square, bias=tmp8[:, 2:3], scale=1.0,
                    accum_out=tmp8[:, 3:4]), waits=[("V", v9)], inc="S")
                v10 = E("vector", lambda e: e.tensor_scalar_mul(
                    tmp8[:, 3:4], tmp8[:, 3:4], INV_D),
                    waits=[("S", s5)], inc="V")
                s6 = E("scalar", lambda e: e.activation(
                    tmp8[:, 3:4], tmp8[:, 3:4], AF.Sqrt, bias=eps_col, scale=1.0),
                    waits=[("V", v10)], inc="S")
                v11 = E("vector", lambda e: e.reciprocal(
                    tmp8[:, 3:4], tmp8[:, 3:4]), waits=[("S", s6)], inc="V")
                v12 = E("vector", lambda e: e.tensor_mul(
                    tmp8[:, 2:3], tmp8[:, 2:3], tmp8[:, 3:4]),
                    waits=[("V", v11)], inc="V")
                x1ln_done_tile[gt] = E("scalar", lambda e, gt=gt: e.activation(
                    bf("X1LN")[:, gt * D:(gt + 1) * D], scr(3), AF.Identity,
                    bias=tmp8[:, 2:3], scale=tmp8[:, 3:4]),
                    waits=[("V", v12)], inc="S")

            ba.free(f"OT{b}")
            ba.free(f"QT{b}")
            ba.free(f"KV{b}")

        x1ln_all = em.val("S")
        attn_T = em.val("T")
        ba.free("PW")
        ba.free("ROWS")
        ba.free("KVST")
        fa.free("XSTG")
        fa.free("WB")
        fa.free("BB")
        fa.free("KBB")
        fa.free("VBB")

        ba.alloc("PHI", 8 * NP, side="right")
        for c in range(8):
            E("gpsimd", lambda e, c=c: e.dma_start(
                out=bf("PHI")[:, c * NP:(c + 1) * NP],
                in_=phi2[c * 128:(c + 1) * 128, :]),
                waits=[("T", o_T[1])] if c == 0 else (),
                inc="wL", amt=16)
        phi_done = em.val("wL")

        ba.alloc("XTS", 2 * D)
        fa.alloc("ESCR", 3 * NP)
        ba.alloc("L1", NT * NP)
        ba.alloc("DIFF", NT * NP)
        ba.alloc("CSTG", NP)

        def xts(i):
            off, _ = ba.lookup("XTS")
            i = i % 2
            return ABF[:, off + i * D: off + (i + 1) * D]

        def escr(i):
            off, _ = fa.lookup("ESCR")
            i = i % 3
            return AF32[:, off + i * NP: off + (i + 1) * NP]

        def cstg():
            off, _ = ba.lookup("CSTG")
            return ABF[:, off:off + NP]

        xt_tp = {}
        logits_tK = {}
        comb_dma = {}

        def emit_xT(seq, gt):
            waits = [("S", x1ln_done_tile[gt]), ("T", attn_T)]
            if seq >= 2:
                waits.append(("T", logits_tK[seq - 2]))
            tsem = f"tpH0{seq % 2}"
            E("sync", lambda e, seq=seq, gt=gt: e.dma_start(
                out=xts(seq).rearrange("p (c q) -> p c q", c=8),
                in_=bf("X1LN").rearrange("p (g c q) -> p g c q", g=16, c=8)
                [:, gt, :, :],
                transpose=True), waits=waits, inc=tsem, amt=16)
            xt_tp[seq] = (tsem, em.val(tsem))

        def logits_mm(seq, pa, pb, extra_waits):
            for c in range(8):
                lhs = xts(seq)[:, c * 128:(c + 1) * 128]
                for n in range(4):
                    psn = pa if n < 2 else pb
                    E("tensor", lambda e, lhs=lhs, psn=psn, c=c, n=n: e.matmul(
                        PS[psn][:, (n % 2) * 512:((n % 2) + 1) * 512], lhs,
                        bf("PHI")[:, c * NP + n * 512: c * NP + (n + 1) * 512],
                        start=(c == 0), stop=(c == 7)),
                        waits=(list(extra_waits) + [xt_tp[seq]])
                        if (c == 0 and n == 0) else ())
            logits_tK[seq] = em.tag_last("tensor", "T")
            return logits_tK[seq]

        sC = 0
        sB = 0
        ebg = {0: 0, 1: 0}
        for t in range(NT):
            seq = t
            emit_xT(seq, NT + t)
            pa, pb = (0, 1) if t % 2 == 0 else (2, 3)
            ga, gb = ps_guard[pa], ps_guard[pb]
            tK = logits_mm(seq, pa, pb, [ga, gb, ("wL", phi_done)])
            eb = escr(t % 2)
            ex = escr(2)
            vA = E("vector", lambda e, eb=eb, pa=pa: e.tensor_copy(
                eb[:, 0:1024], PS[pa][:, :]),
                waits=[("T", tK)] + ([("S", sB)] if sB else []) +
                ([("G", ebg[t % 2])] if ebg[t % 2] else []), inc="V")
            vB = E("vector", lambda e, eb=eb, pb=pb: e.tensor_copy(
                eb[:, 1024:2048], PS[pb][:, :]), waits=[("V", vA)], inc="V")
            ps_guard[pa] = ("V", vB)
            ps_guard[pb] = ("V", vB)
            vL = E("gpsimd", lambda e, t=t, eb=eb: e.tensor_copy(
                bf("L1")[:, t * NP:(t + 1) * NP], eb),
                waits=[("V", vB)], inc="G")
            ebg[t % 2] = vL
            vC = E("vector", lambda e, eb=eb: e.reduce_max(
                smax[:, 0:1], eb[:, 0:1024], axis=AX.X),
                waits=[("V", vB)], inc="V")
            vD = E("vector", lambda e, eb=eb: e.reduce_max(
                smax[:, 1:2], eb[:, 1024:2048], axis=AX.X),
                waits=[("V", vC)], inc="V")
            vE = E("vector", lambda e: e.tensor_max(
                smax[:, 0:1], smax[:, 0:1], smax[:, 1:2]),
                waits=[("V", vD)], inc="V")
            vF = E("vector", lambda e: e.tensor_scalar_mul(
                smax[:, 1:2], smax[:, 0:1], -1.0), waits=[("V", vE)], inc="V")
            sA = E("scalar", lambda e, eb=eb, ex=ex: e.activation(
                ex[:, 0:1024], eb[:, 0:1024], AF.Exp, bias=smax[:, 1:2],
                scale=1.0, accum_out=smax[:, 2:3]),
                waits=[("V", vF)] + ([("S", sC)] if sC else []), inc="S")
            sB = E("scalar", lambda e, eb=eb, ex=ex: e.activation(
                ex[:, 1024:2048], eb[:, 1024:2048], AF.Exp, bias=smax[:, 1:2],
                scale=1.0, accum_out=smax[:, 3:4]), inc="S")
            vG = E("vector", lambda e: e.tensor_add(
                smax[:, 2:3], smax[:, 2:3], smax[:, 3:4]),
                waits=[("S", sB)], inc="V")
            vH = E("vector", lambda e: e.reciprocal(
                smax[:, 3:4], smax[:, 2:3]), waits=[("V", vG)], inc="V")
            sC = E("scalar", lambda e, ex=ex: e.activation(
                cstg(), ex, AF.Identity, bias=0.0, scale=smax[:, 3:4]),
                waits=[("V", vH)] +
                ([("cD", comb_dma[t - 1])] if t >= 1 else []), inc="S")
            comb_dma[t] = E("gpsimd", lambda e, t=t: e.dma_start(
                out=comb_d[TPB + t * 128: TPB + (t + 1) * 128, :],
                in_=cstg()), waits=[("S", sC)], inc="cD", amt=16)

        for t in range(NT):
            seq = NT + t
            emit_xT(seq, t)
            pa, pb = (0, 1) if t % 2 == 0 else (2, 3)
            ga, gb = ps_guard[pa], ps_guard[pb]
            tK = logits_mm(seq, pa, pb, [ga, gb])
            eb = escr(t % 2)
            ex = escr(2)
            vA = E("vector", lambda e, eb=eb, pa=pa: e.tensor_copy(
                eb[:, 0:1024], PS[pa][:, :]),
                waits=[("T", tK), ("S", sB), ("G", ebg[t % 2])], inc="V")
            vB = E("vector", lambda e, eb=eb, pb=pb: e.tensor_copy(
                eb[:, 1024:2048], PS[pb][:, :]), waits=[("V", vA)], inc="V")
            ps_guard[pa] = ("V", vB)
            ps_guard[pb] = ("V", vB)
            vDf = E("gpsimd", lambda e, t=t, eb=eb: e.tensor_sub(
                bf("DIFF")[:, t * NP:(t + 1) * NP], eb,
                bf("L1")[:, t * NP:(t + 1) * NP]),
                waits=[("V", vB)], inc="G")
            ebg[t % 2] = vDf
            vC = E("vector", lambda e, eb=eb: e.reduce_max(
                smax[:, 4:5], eb[:, 0:1024], axis=AX.X),
                waits=[("V", vB)], inc="V")
            vD = E("vector", lambda e, eb=eb: e.reduce_max(
                smax[:, 5:6], eb[:, 1024:2048], axis=AX.X),
                waits=[("V", vC)], inc="V")
            vE = E("vector", lambda e: e.tensor_max(
                smax[:, 4:5], smax[:, 4:5], smax[:, 5:6]),
                waits=[("V", vD)], inc="V")
            vF = E("vector", lambda e: e.tensor_scalar_mul(
                smax[:, 5:6], smax[:, 4:5], -1.0), waits=[("V", vE)], inc="V")
            sA = E("scalar", lambda e, eb=eb, ex=ex: e.activation(
                ex[:, 0:1024], eb[:, 0:1024], AF.Exp, bias=smax[:, 5:6],
                scale=1.0, accum_out=smax[:, 6:7]),
                waits=[("V", vF), ("S", sC)], inc="S")
            sB = E("scalar", lambda e, eb=eb, ex=ex: e.activation(
                ex[:, 1024:2048], eb[:, 1024:2048], AF.Exp, bias=smax[:, 5:6],
                scale=1.0, accum_out=smax[:, 7:8]), inc="S")
            vG = E("vector", lambda e: e.tensor_add(
                smax[:, 6:7], smax[:, 6:7], smax[:, 7:8]),
                waits=[("S", sB)], inc="V")
            vH = E("vector", lambda e: e.reciprocal(
                smax[:, 7:8], smax[:, 6:7]), waits=[("V", vG)], inc="V")
            sC = E("scalar", lambda e, ex=ex: e.activation(
                cstg(), ex, AF.Identity, bias=0.0, scale=smax[:, 7:8]),
                waits=[("V", vH), ("cD", comb_dma[NT + t - 1])], inc="S")
            comb_dma[NT + t] = E("gpsimd", lambda e, t=t: e.dma_start(
                out=comb_d[t * 128:(t + 1) * 128, :], in_=cstg()),
                waits=[("S", sC)], inc="cD", amt=16)
        comb_dma_done = em.val("cD")
        l0_done = em.val("S")
        diff_last_v = em.val("G")

        ba.free("PHI")
        ba.free("XTS")
        fa.free("ESCR")
        ba.free("CSTG")

        d0 = bf("L1")
        d1 = bf("DIFF")
        disp_done = 0
        for t in range(NT):
            s = E("scalar", lambda e, t=t: e.activation(
                d0[:, t * NP:(t + 1) * NP],
                bf("DIFF")[:, t * NP:(t + 1) * NP], AF.Sigmoid,
                bias=0.0, scale=1.0),
                waits=[("S", l0_done), ("G", diff_last_v)] if t == 0 else (),
                inc="S")
            disp_done = E("vector", lambda e, t=t, s=s: e.tensor_scalar(
                d1[:, t * NP:(t + 1) * NP], d0[:, t * NP:(t + 1) * NP],
                -1.0, 1.0, ALU.mult, ALU.add), waits=[("S", s)], inc="V")

        ba.alloc("SLST", 2 * D)

        def slst(i):
            off, _ = ba.lookup("SLST")
            i = i % 2
            return ABF[:, off + i * D: off + (i + 1) * D]

        sl_dma = {}
        slots_tK = {}
        idx = 0
        for b in (0, 1):
            dt_ = d0 if b == 0 else d1
            for j in range(16):
                psn = j % 2
                g = ps_guard[psn]
                for t in range(NT):
                    lhs = dt_[:, t * NP + j * 128: t * NP + j * 128 + 128]
                    for n in range(2):
                        E("tensor", lambda e, lhs=lhs, psn=psn, t=t, n=n, b=b:
                          e.matmul(
                            PS[psn][:, n * 512:(n + 1) * 512], lhs,
                            bf("X1LN")[:, (b * NT + t) * D + n * 512:
                                       (b * NT + t) * D + (n + 1) * 512],
                            start=(t == 0), stop=(t == NT - 1)),
                            waits=[g, ("V", disp_done)]
                            if (t == 0 and n == 0) else ())
                tK = em.tag_last("tensor", "T")
                ssem = f"slD{2 * b + idx % 2}"
                vv = E("vector", lambda e, idx=idx, psn=psn: e.tensor_copy(
                    slst(idx), PS[psn][:, :]),
                    waits=[("T", tK)] +
                    ([sl_dma[idx - 2]] if idx >= 2 else []), inc="V")
                ps_guard[psn] = ("V", vv)
                row = (j // 2) * 256 + (j % 2) * 128
                dv = E("scalar", lambda e, row=row, idx=idx, b=b:
                       e.dma_start(out=sl_rs_in[b][row:row + 128, :],
                                   in_=slst(idx)),
                       waits=[("V", vv)], inc=ssem, amt=16)
                sl_dma[idx] = (ssem, dv)
                idx += 1
            slots_tK[b] = em.val("T")

            if b == 0:
                rs0 = E("gpsimd", lambda e: e.collective_compute(
                    "ReduceScatter", ALU.add, replica_groups=rg,
                    ins=[sl_rs_in[0]], outs=[sl_rs_out[0]]),
                    waits=[("slD0", em.val("slD0")),
                           ("slD1", em.val("slD1"))], inc="C")
        rs1 = E("gpsimd", lambda e: e.collective_compute(
            "ReduceScatter", ALU.add, replica_groups=rg,
            ins=[sl_rs_in[1]], outs=[sl_rs_out[1]]),
            waits=[("slD2", em.val("slD2")), ("slD3", em.val("slD3"))],
            inc="C")
        rs_done = {0: rs0, 1: rs1}

        ba.free("L1")
        ba.free("DIFF")
        ba.free("X1LN")
        ba.free("SLST")

        ba.alloc("SLBF", 2 * D)
        ba.alloc("SLT0", 8 * 256)
        ba.alloc("SLT1", 8 * 256)
        ba.alloc("ROWS2", 8 * D)
        ba.alloc("WBUF", 2 * 8 * D)
        ba.alloc("YST", 2 * D)

        def slbf(i):
            off, _ = ba.lookup("SLBF")
            return ABF[:, off + (i % 2) * D: off + (i % 2) * D + D]

        def rows2():
            off, _ = ba.lookup("ROWS2")
            return ABF[0:1, off:off + 8 * D]

        def wbuf(i):
            off, _ = ba.lookup("WBUF")
            i = i % 2
            return ABF[:, off + i * 8 * D: off + (i + 1) * 8 * D]

        def yst(i):
            off, _ = ba.lookup("YST")
            i = i % 2
            return ABF[:, off + i * D: off + (i + 1) * D]

        E("gpsimd", lambda e: e.dma_start(
            out=rows2(), in_=fb_in.rearrange("(o n) e -> o (n e)", o=1)),
            waits=[("T", slots_tK[0])], inc="wL", amt=16)
        fb_done = em.val("wL")
        wb_load = {}
        for k in (0, 1):
            wb_load[k] = (f"wb{k % 2}", E(
                "gpsimd", lambda e, k=k: e.dma_start(
                    out=wbuf(k).rearrange("p (c d) -> p c d", d=D),
                    in_=w_exp[k].rearrange("(c p) d -> p c d", p=128)),
                waits=[("T", slots_tK[0])] if k == 0 else (),
                inc=f"wb{k % 2}", amt=16))

        yT = {}
        y_dma = {}
        slt_done = {}
        prev_slt = ("tpX", 0)
        for b in (0, 1):
            sltn = f"SLT{b}"
            for i in range(2):
                E("sync", lambda e, i=i, b=b: e.dma_start(
                    out=slbf(i), in_=sl_rs_out[b][i * 128:(i + 1) * 128, :]),
                    waits=[("C", rs_done[b]), prev_slt,
                           ("slD2", em.val("slD2")),
                           ("slD3", em.val("slD3"))] if i == 0 else (),
                    inc="slL", amt=16)
            slL_all = em.val("slL")
            for i in range(2):
                E("sync", lambda e, i=i, sltn=sltn: e.dma_start(
                    out=bf(sltn).rearrange("p (c q) -> p c q", c=8)
                    [:, :, i * 128:(i + 1) * 128],
                    in_=slbf(i).rearrange("p (c q) -> p c q", c=8),
                    transpose=True),
                    waits=[("slL", slL_all)] if i == 0 else (),
                    inc="tpX", amt=16)
            slt_done[b] = ("tpX", em.val("tpX"))
            prev_slt = slt_done[b]

        for n in range(NEXP):
            for bb in (0, 1):
                sltn = f"SLT{bb}"
                psn = 2 * (n % 2) + bb
                g = ps_guard[psn]
                for ns in range(2):
                    E("tensor", lambda e, n=n, ns=ns, psn=psn: e.matmul(
                        PS[psn][0:32, ns * 512:(ns + 1) * 512],
                        ones()[:, 0:32],
                        rows2()[:, n * D + ns * 512: n * D + (ns + 1) * 512],
                        start=True, stop=False),
                        waits=[g, wb_load[n], slt_done[bb], ("wL", fb_done)]
                        if ns == 0 else ())
                    for c in range(8):
                        E("tensor", lambda e, n=n, ns=ns, c=c, psn=psn,
                          sltn=sltn: e.matmul(
                            PS[psn][0:32, ns * 512:(ns + 1) * 512],
                            bf(sltn)[:, c * 256 + n * 32:
                                     c * 256 + (n + 1) * 32],
                            wbuf(n)[:, c * D + ns * 512:
                                    c * D + (ns + 1) * 512],
                            start=False, stop=(c == 7)))
            yT[n] = em.tag_last("tensor", "T")
            if n + 2 < NEXP:
                wb_load[n + 2] = (f"wb{n % 2}", E(
                    "gpsimd", lambda e, n=n: e.dma_start(
                        out=wbuf(n + 2).rearrange("p (c d) -> p c d", d=D),
                        in_=w_exp[n + 2].rearrange("(c p) d -> p c d", p=128)),
                    waits=[("T", yT[n])], inc=f"wb{n % 2}", amt=16))
            ysem = f"yD{n % 2}"
            pa = 2 * (n % 2)
            v0 = E("scalar", lambda e, n=n, pa=pa: e.activation(
                yst(n)[0:32, :], PS[pa][0:32, :], AF.Identity, bias=0.0,
                scale=1.0),
                waits=[("T", yT[n])] +
                ([y_dma[n - 2]] if n >= 2 else []), inc="S")
            v1 = E("scalar", lambda e, n=n, pa=pa: e.activation(
                yst(n)[32:64, :], PS[pa + 1][0:32, :], AF.Identity, bias=0.0,
                scale=1.0), waits=[("S", v0)], inc="S")
            ps_guard[pa] = ("S", v1)
            ps_guard[pa + 1] = ("S", v1)
            for bb in (0, 1):
                dv = E("scalar", lambda e, n=n, bb=bb: e.dma_start(
                    out=y_ag_in[bb * 256 + n * 32: bb * 256 + (n + 1) * 32, :],
                    in_=yst(n)[bb * 32:(bb + 1) * 32, :]),
                    waits=[("S", v1)] if bb == 0 else (), inc=ysem, amt=16)
            y_dma[n] = (ysem, dv)

        ag_done = E("gpsimd", lambda e: e.collective_compute(
            "AllGather", ALU.bypass, replica_groups=rg,
            ins=[y_ag_in], outs=[y_ag_out]),
            waits=[("yD0", em.val("yD0")), ("yD1", em.val("yD1"))],
            inc="C")

        ba.alloc("YB", 16 * D)
        ba.alloc("CT", 16 * D)
        fa.alloc("X1R", 2 * D)
        fa.alloc("OST", 2 * D)
        fa.alloc("OSNK", D)

        def x1r(i):
            off, _ = fa.lookup("X1R")
            i = i % 2
            return AF32[:, off + i * D: off + (i + 1) * D]

        def ost(i):
            off, _ = fa.lookup("OST")
            i = i % 2
            return AF32[:, off + i * D: off + (i + 1) * D]

        out_dma = {}
        xr_done = {}
        v7_done = {}
        prev_T = {0: slots_tK[1], 1: 0}
        def gsem(base, grp):
            return base if grp == 0 else f"{base}{grp}"

        for b in (0, 1):
            tb = b * NT
            yb_j = {}
            ct_j = {}
            for j in range(16):
                row = (j // 2) * 512 + b * 256 + (j % 2) * 128
                sn = gsem("yB", j // 4)
                E("gpsimd", lambda e, j=j, row=row: e.dma_start(
                    out=bf("YB")[:, j * D:(j + 1) * D],
                    in_=y_ag_out[row:row + 128, :]),
                    waits=[("C", ag_done), ("T", prev_T[b]),
                           ("cD", comb_dma_done)]
                    if j == 0 else (), inc=sn, amt=16)
                yb_j[j // 4] = (sn, em.val(sn))
            for j in range(16):
                sn = gsem("tpC", j // 4)
                E("sync", lambda e, j=j, b=b: e.dma_start(
                    out=bf("CT")[:, j * D:(j + 1) * D],
                    in_=comb_d[b * TPB:(b + 1) * TPB, j * 128:(j + 1) * 128],
                    transpose=True),
                    waits=[("cD", comb_dma_done), ("T", prev_T[b])]
                    if j == 0 else (), inc=sn, amt=16)
                ct_j[j // 4] = (sn, em.val(sn))

            for t in range(NT):
                gt = tb + t
                psn = t % 4
                g = ps_guard[psn]
                xr_done[gt] = (f"x1R{gt % 2}", E(
                    "gpsimd", lambda e, gt=gt: e.dma_start(
                        out=x1r(gt), in_=x1_d[gt * 128:(gt + 1) * 128, :]),
                    waits=[("spD", x1_spill[gt])] +
                    ([("V", v7_done[gt - 2])] if gt >= 2 else []),
                    inc=f"x1R{gt % 2}", amt=16))
                for j in range(16):
                    lhs = bf("CT")[:, j * D + t * 128: j * D + t * 128 + 128]
                    if t == 0 and j % 4 == 0:
                        w0 = [yb_j[j // 4], ct_j[j // 4]]
                        if j == 0:
                            w0 = [g] + w0
                    elif t > 0 and j == 0:
                        w0 = [g]
                    else:
                        w0 = ()
                    for n in range(2):
                        E("tensor", lambda e, lhs=lhs, psn=psn, j=j, n=n:
                          e.matmul(
                            PS[psn][:, n * 512:(n + 1) * 512], lhs,
                            bf("YB")[:, j * D + n * 512: j * D + (n + 1) * 512],
                            start=(j == 0), stop=(j == 15)),
                            waits=w0 if n == 0 else ())
                        w0 = ()
                tK = em.tag_last("tensor", "T")
                ev = scr(t % 2)
                vE = E("vector", lambda e, ev=ev, psn=psn: e.tensor_copy(
                    ev, PS[psn][:, :]),
                    waits=[("T", tK)] +
                    ([("V", v7_done[gt - 2])] if gt >= 2 else []), inc="V")
                ps_guard[psn] = ("V", vE)
                s2 = E("scalar", lambda e, ev=ev: e.activation(
                    bf("SINK"), ev, AF.Square, bias=0.0,
                    scale=1.0, accum_out=tmp8[:, 5:6]),
                    waits=[("V", vE)], inc="S")
                v2 = E("vector", lambda e, ev=ev: e.reduce_sum(
                    tmp8[:, 4:5], ev, axis=AX.X, negate=True),
                    waits=[("V", vE)], inc="V")
                v3 = E("vector", lambda e: e.tensor_scalar_mul(
                    tmp8[:, 4:5], tmp8[:, 4:5], INV_D),
                    waits=[("V", v2)], inc="V")
                v3b = E("vector", lambda e: e.tensor_mul(
                    tmp8[:, 9:10], tmp8[:, 4:5], tmp8[:, 4:5]),
                    waits=[("V", v3)], inc="V")
                v4 = E("vector", lambda e: e.tensor_scalar_mul(
                    tmp8[:, 5:6], tmp8[:, 5:6], INV_D),
                    waits=[("S", s2), ("V", v3b)], inc="V")
                v4b = E("vector", lambda e: e.tensor_sub(
                    tmp8[:, 5:6], tmp8[:, 5:6], tmp8[:, 9:10]),
                    waits=[("V", v4)], inc="V")
                s3 = E("scalar", lambda e: e.activation(
                    tmp8[:, 5:6], tmp8[:, 5:6], AF.Sqrt, bias=eps_col,
                    scale=1.0), waits=[("V", v4b)], inc="S")
                v5 = E("vector", lambda e: e.reciprocal(
                    tmp8[:, 5:6], tmp8[:, 5:6]), waits=[("S", s3)], inc="V")
                v6 = E("vector", lambda e: e.tensor_mul(
                    tmp8[:, 4:5], tmp8[:, 4:5], tmp8[:, 5:6]),
                    waits=[("V", v5)], inc="V")
                sX = E("scalar", lambda e, gt=gt, t=t: e.activation(
                    scr(2 + t % 2), x1r(gt), AF.Identity,
                    bias=tmp8[:, 4:5], scale=1.0),
                    waits=[("V", v6), xr_done[gt],
                           ("spD", x1_spill[2 * NT - 1])], inc="S")
                v7 = E("vector", lambda e, gt=gt, t=t, ev=ev:
                       e.scalar_tensor_tensor(
                        ost(gt), ev, tmp8[:, 5:6], scr(2 + t % 2),
                        ALU.mult, ALU.add),
                    waits=[("S", sX)] +
                    ([(f"O{gt % 2}", out_dma[gt - 2])] if gt >= 2 else []),
                    inc="V")
                v7_done[gt] = v7
                out_dma[gt] = E("scalar", lambda e, gt=gt: e.dma_start(
                    out=out_d[gt * 128:(gt + 1) * 128, :], in_=ost(gt)),
                    waits=[("V", v7)], inc=f"O{gt % 2}", amt=16)
            prev_T[1] = em.val("T")

        final_o = [("O0", em.val("O0")), ("O1", em.val("O1"))]
        em.wait("sync", final_o)
        em.wait("gpsimd", final_o)

        @block.sync
        def _(eng):
            for fn in em.ops["sync"]:
                fn(eng, sems)

        @block.scalar
        def _(eng):
            for fn in em.ops["scalar"]:
                fn(eng, sems)

        @block.vector
        def _(eng):
            for fn in em.ops["vector"]:
                fn(eng, sems)

        @block.tensor
        def _(eng):
            for fn in em.ops["tensor"]:
                fn(eng, sems)

        @block.gpsimd
        def _(eng):
            for fn in em.ops["gpsimd"]:
                fn(eng, sems)

    return nc


_CACHE = {}


def kernel(L0_attn_x, L0_attn_norm_w, L0_attn_norm_b, L0_attn_qkv_w,
           L0_attn_qkv_b, L0_attn_proj_w, L0_attn_proj_b, L0_ffn_phi,
           L0_ffn_W, L0_ffn_b):
    x = np.asarray(L0_attn_x, dtype=np.float32)
    qkv_wT = np.ascontiguousarray(np.asarray(L0_attn_qkv_w, np.float32).T)
    qkv_b_ = np.asarray(L0_attn_qkv_b, np.float32).reshape(1, 3 * D)
    proj_wT = np.ascontiguousarray(np.asarray(L0_attn_proj_w, np.float32).T)
    proj_b_ = np.asarray(L0_attn_proj_b, np.float32).reshape(1, D)
    norm_w_ = np.asarray(L0_attn_norm_w, np.float32).reshape(1, D)
    norm_b_ = np.asarray(L0_attn_norm_b, np.float32).reshape(1, D)
    phi2_ = np.ascontiguousarray(
        np.asarray(L0_ffn_phi, np.float32).reshape(D, NP))
    W = np.asarray(L0_ffn_W, np.float32)
    fbv = np.asarray(L0_ffn_b, np.float32)

    if "nc" not in _CACHE:
        _CACHE["nc"] = build_nc()
    nc = _CACHE["nc"]

    in_maps = []
    for c in range(NCORES):
        xc = np.ascontiguousarray(
            x[:, c * TPB:(c + 1) * TPB, :].reshape(2 * TPB, D))
        in_maps.append({
            "x_in": xc, "qkv_wT": qkv_wT, "qkv_b": qkv_b_,
            "proj_wT": proj_wT, "proj_b": proj_b_,
            "norm_w": norm_w_, "norm_b": norm_b_, "phi2": phi2_,
            "w_exp": np.ascontiguousarray(W[c * NEXP:(c + 1) * NEXP]),
            "fb_in": np.ascontiguousarray(fbv[c * NEXP:(c + 1) * NEXP]),
        })

    res = run_bass_kernel_spmd(nc, in_maps, core_ids=list(range(NCORES)))

    out = np.empty((2, 8192, D), np.float32)
    for c in range(NCORES):
        r = np.asarray(res.results[c]["out"])
        out[0, c * TPB:(c + 1) * TPB] = r[:TPB]
        out[1, c * TPB:(c + 1) * TPB] = r[TPB:]
    return out
